# version 15
# baseline (speedup 1.0000x reference)
import contextlib
import os
import numpy as np
import ml_dtypes

import concourse.bass as bass
import concourse.mybir as mybir
from concourse.bass_utils import run_bass_kernel_spmd

F32 = mybir.dt.float32
BF16 = mybir.dt.bfloat16
AF = mybir.ActivationFunctionType
ALU = mybir.AluOpType
AX = mybir.AxisListType

NCORES = 8
DM = 1024
H = 16
DH = 64
CH = 128
NLVL = 4
EPS_LN = 1e-5
EPS_LA = 1e-6
NEG = -1.0e30
BF = ml_dtypes.bfloat16



ENGINES = ("PE", "ACT", "DVE", "POOL", "SP")


class Op:
    __slots__ = ("engine", "kind", "fn", "deps", "marked", "seq", "idx", "slot")

    def __init__(self, engine, kind, fn, deps, idx):
        self.engine, self.kind, self.fn = engine, kind, fn
        self.deps, self.idx = deps, idx
        self.marked = False
        self.seq = None
        self.slot = None


class Buf:
    __slots__ = ("name", "writers", "readers")

    def __init__(self, name):
        self.name = name
        self.writers = {}
        self.readers = {}

    def _conf(self, table, key):
        out = []
        if key is None:
            for ops in table.values():
                out.extend(ops)
        else:
            out.extend(table.get(key, ()))
            out.extend(table.get(None, ()))
        return out

    def dep_read(self, key):
        return self._conf(self.writers, key)

    def dep_write(self, key):
        return self._conf(self.writers, key) + self._conf(self.readers, key)

    def note_read(self, key, op):
        self.readers.setdefault(key, []).append(op)

    def note_write(self, key, op):
        if key is None:
            self.writers.clear()
            self.readers.clear()
        else:
            self.writers.pop(key, None)
            self.readers.pop(key, None)
        self.writers.setdefault(key, []).append(op)


def _norm(items):
    return [it if isinstance(it, tuple) else (it, None) for it in items]


class Sched:
    def __init__(self, nc):
        self.nc = nc
        self.ops = []
        self.pending = []
        self.barrier_deps = []
        self.cnt = {e: 0 for e in ENGINES}
        self.dma_cnt = {e: 0 for e in ENGINES}
        self.dma_use = {}
        self.dma_last = {}
        self.sems = {}
        self.dma_sems = {}
        self.hw = {e: {} for e in ENGINES}
        self.flushed_upto = 0
        self.flushed_last = {}
        self.NSLOT = 4

    def init_sems(self, stk):
        for e in ENGINES:
            self.sems[e] = stk.enter_context(self.nc.semaphore(f"sem_{e}"))
            for sl in range(self.NSLOT):
                self.dma_sems[(e, sl)] = stk.enter_context(
                    self.nc.semaphore(f"dsem_{e}{sl}"))
                self.dma_use[(e, sl)] = 0

    def _fix(self, d):
        if d.idx < self.flushed_upto and not d.marked:
            key = (d.engine, d.kind) if d.kind == "c" else (d.engine, "d", d.slot)
            return self.flushed_last[key]
        return d

    def add(self, engine, fn, reads=(), writes=(), kind="c", deps=()):
        out, seen = [], set()

        def push(d):
            d = self._fix(d)
            if id(d) not in seen:
                seen.add(id(d))
                out.append(d)

        for b, k in _norm(reads):
            for d in b.dep_read(k):
                push(d)
        for b, k in _norm(writes):
            for d in b.dep_write(k):
                push(d)
        for d in deps:
            push(d)
        for d in self.barrier_deps:
            push(d)
        op = Op(engine, kind, fn, out, len(self.ops))
        self.ops.append(op)
        self.pending.append(op)
        for b, k in _norm(reads):
            b.note_read(k, op)
        for b, k in _norm(writes):
            b.note_write(k, op)
        return op

    def pe(self, fn, reads=(), writes=(), deps=()):
        return self.add("PE", fn, reads, writes, "c", deps)

    def act(self, fn, reads=(), writes=(), deps=()):
        return self.add("ACT", fn, reads, writes, "c", deps)

    def dve(self, fn, reads=(), writes=(), deps=()):
        return self.add("DVE", fn, reads, writes, "c", deps)

    def pool(self, fn, reads=(), writes=(), deps=()):
        return self.add("POOL", fn, reads, writes, "c", deps)

    def dma(self, q, fn, reads=(), writes=(), deps=()):
        return self.add(q, fn, reads, writes, "d", deps)

    def barrier(self):
        assert not self.pending, "barrier must follow a flush"
        last = {}
        for op in self.ops:
            key = (op.engine, "c") if op.kind == "c" else (op.engine, "d", op.slot)
            last[key] = op
        self.barrier_deps = list(last.values())

    def flush(self, engmap):
        pend, self.pending = self.pending, []
        if not pend:
            return
        for op in pend:
            if op.kind == "d":
                op.marked = True
            best = {}
            dd = []
            for d in op.deps:
                if d.kind == "c":
                    k = d.engine
                    if k not in best or best[k].idx < d.idx:
                        best[k] = d
                else:
                    dd.append(d)
            op.deps = dd + list(best.values())
            for d in op.deps:
                d.marked = True
        last = {}
        for op in pend:
            last[(op.engine, op.kind)] = op
        for k, op in last.items():
            op.marked = True
        for op in pend:
            if op.marked:
                if op.kind == "d":
                    sl = self.dma_cnt[op.engine] % self.NSLOT
                    self.dma_cnt[op.engine] += 1
                    op.slot = sl
                    self.dma_use[(op.engine, sl)] += 1
                    op.seq = 16 * self.dma_use[(op.engine, sl)]
                    prev = self.dma_last.get((op.engine, sl))
                    if prev is not None and prev not in op.deps:
                        op.deps.append(prev)
                    self.dma_last[(op.engine, sl)] = op
                else:
                    self.cnt[op.engine] += 1
                    op.seq = self.cnt[op.engine]
        self.flushed_upto = len(self.ops)
        for op in pend:
            key = (op.engine, "c") if op.kind == "c" else (op.engine, "d", op.slot)
            if op.marked:
                self.flushed_last[key] = op
        for ename in ENGINES:
            eops = [op for op in pend if op.engine == ename]
            if not eops:
                continue

            def run(eng, eops=eops, ename=ename):
                hw = self.hw[ename]
                for op in eops:
                    need = {}
                    for d in op.deps:
                        if d.seq is None:
                            raise RuntimeError("unmarked dep")
                        if d.kind == "d":
                            key = ("d", d.engine, d.slot)
                        else:
                            if (d.engine == ename == "PE" and op.kind == "c"):
                                continue
                            key = ("c", d.engine)
                        if need.get(key, 0) < d.seq:
                            need[key] = d.seq
                    for key, seq in need.items():
                        if hw.get(key, 0) < seq:
                            if key[0] == "d":
                                eng.wait_ge(self.dma_sems[(key[1], key[2])], seq)
                            else:
                                eng.wait_ge(self.sems[key[1]], seq)
                            hw[key] = seq
                    inst = op.fn(eng)
                    if op.marked:
                        if op.kind == "d":
                            inst.then_inc(self.dma_sems[(ename, op.slot)], 16)
                        else:
                            inst.then_inc(self.sems[ename], 1)

            engmap[ename](run)



def host_consts():
    c = {}
    c["ident_bf"] = np.eye(CH, dtype=BF)
    c["tril_bf"] = (np.arange(CH)[:, None] <= np.arange(CH)[None, :]).astype(BF)
    tk = np.arange(CH)[:, None]
    tq = np.arange(CH)[None, :]
    c["mask_diag"] = np.where((tk <= tq) & (tk >= tq - 63), 0.0, NEG).astype(np.float32)
    tkp = np.arange(64)[:, None]
    mp = np.where(tkp >= tq + 1, 0.0, NEG).astype(np.float32)
    c["mask_prev"] = np.concatenate([mp, mp], axis=0)
    c["mask_prev0"] = np.full((CH, CH), NEG, dtype=np.float32)
    mts = []
    for lvl in range(NLVL):
        blk = 2 ** (lvl + 1)
        t_ = np.arange(CH)
        start = (t_ // blk) * blk
        M = np.zeros((CH, CH), dtype=np.float64)
        for tp in range(CH):
            M[tp, start[tp]:tp + 1] = 1.0 / (tp - start[tp] + 1)
        mts.append(M.T.astype(BF))
    c["mt_bf"] = np.ascontiguousarray(np.stack(mts))
    c["ones_col_bf"] = np.ones((CH, 1), dtype=BF)
    c["ones_row_f"] = np.ones((1, CH), dtype=np.float32)
    c["ones_row_bf"] = np.ones((1, CH), dtype=BF)
    c["one_f"] = np.ones((1, 1), dtype=np.float32)
    return c



def build_program(N=4096):
    nc = bass.Bass("TRN2", target_bir_lowering=False)
    NSUPA = N // 256
    NSUPB = N // 512
    CT = 1024 if N >= 1024 else N
    NSUPC = N // CT

    dt = {}

    def din(name, shape, dtype=F32):
        dt[name] = nc.dram_tensor(name, list(shape), dtype, kind="ExternalInput").ap()
        return dt[name]

    x_d = din("x", (N, DM))
    for w in ("wqT", "wkT", "wvT", "wklT", "wvlT", "woT"):
        din(w, (DM, DM))
    din("wgT", (2 * DM, DM))
    din("wgoT", (128, 8))
    din("haarWkT", (NLVL, DH, DH))
    din("haarWvT", (NLVL, DH, DH))
    din("haar_scale", (1, NLVL))
    din("bg", (128, 8))
    din("bgo", (1, 1))
    din("bo_row", (1, DM))
    din("w1T", (DM, 4096))
    din("w2T", (DM, 4096))
    din("h1T", (4096, 4096))
    din("h2T", (4096, 4096))
    din("w3T", (4096, DM))
    din("w1s", (128, 32))
    din("w2s", (128, 32))
    din("h1s", (128, 32))
    din("h2s", (128, 32))
    din("w3s_row", (1, DM))
    din("w3s_bc", (128, DM))
    din("bo_bc", (128, DM))
    din("ident_bf", (CH, CH), BF16)
    din("tril_bf", (CH, CH), BF16)
    din("mask_diag", (CH, CH))
    din("mask_prev", (CH, CH))
    din("mask_prev0", (CH, CH))
    din("mt_bf", (NLVL, CH, CH), BF16)
    din("ones_col_bf", (CH, 1), BF16)
    din("ones_row_f", (1, CH))
    din("ones_row_bf", (1, CH), BF16)
    din("one_f", (1, 1))

    out_d = nc.dram_tensor("out", [N, DM], F32, kind="ExternalOutput").ap()

    xn_scr = nc.dram_tensor("xn_scr", [8, 128, N], BF16, kind="Internal").ap()
    glob_scr = nc.dram_tensor("glob_scr", [N, DM], BF16, kind="Internal").ap()
    local_scr = nc.dram_tensor("local_scr", [N, DM], BF16, kind="Internal").ap()
    wp_scr = {w: nc.dram_tensor(f"wp_{w}", [8, 128, DM], BF16, kind="Internal").ap()
              for w in ("wq", "wk", "wv", "wkl", "wvl")}
    wg_scr = nc.dram_tensor("wg_scr", [16, 128, DM], BF16, kind="Internal").ap()
    wo_scr = nc.dram_tensor("wo_scr", [8, 128, DM], BF16, kind="Internal").ap()
    w1_scr = nc.dram_tensor("w1_scr", [8, 128, 4096], BF16, kind="Internal").ap()
    w2_scr = nc.dram_tensor("w2_scr", [8, 128, 4096], BF16, kind="Internal").ap()
    h1_scr = nc.dram_tensor("h1_scr", [32, 128, 4096], BF16, kind="Internal").ap()
    h2_scr = nc.dram_tensor("h2_scr", [32, 128, 4096], BF16, kind="Internal").ap()
    w3_scr = nc.dram_tensor("w3_scr", [32, 128, DM], BF16, kind="Internal").ap()

    dB = {n: Buf("d_" + n) for n in
          ["xn_scr", "glob_scr", "local_scr", "wg_scr", "wo_scr",
           "w1_scr", "w2_scr", "h1_scr", "h2_scr", "w3_scr", "out",
           "wq", "wk", "wv", "wkl", "wvl"]}

    with contextlib.ExitStack() as top:
        E = top.enter_context
        sch = Sched(nc)
        sch.init_sems(top)

        pb = [E(nc.psum_tensor(f"pb{i}", [128, 512], F32)) for i in range(8)]
        pB = [Buf(f"pb{i}") for i in range(8)]

        ident = E(nc.sbuf_tensor("ident", [CH, CH], BF16))
        onesc = E(nc.sbuf_tensor("onesc", [CH, 1], BF16))
        onef = E(nc.sbuf_tensor("onef", [1, 1], F32))
        swb = E(nc.sbuf_tensor("swb", [128, NLVL], F32))
        bg_sb = E(nc.sbuf_tensor("bg_sb", [128, 8], F32))
        bgo_sb = E(nc.sbuf_tensor("bgo_sb", [1, 1], F32))
        wgoT_sb = E(nc.sbuf_tensor("wgoT_sb", [128, 8], F32))
        wgoT_bf = E(nc.sbuf_tensor("wgoT_bf", [128, 8], BF16))
        s_w1s = E(nc.sbuf_tensor("s_w1s", [128, 32], F32))
        s_w2s = E(nc.sbuf_tensor("s_w2s", [128, 32], F32))
        s_h1s = E(nc.sbuf_tensor("s_h1s", [128, 32], F32))
        s_h2s = E(nc.sbuf_tensor("s_h2s", [128, 32], F32))
        h1s_e = E(nc.sbuf_tensor("h1s_e", [128, 32], F32))
        h2s_e = E(nc.sbuf_tensor("h2s_e", [128, 32], F32))
        cvf = [E(nc.sbuf_tensor(f"cvf{i}", [128, 512], F32)) for i in range(2)]
        cvb = [E(nc.sbuf_tensor(f"cvb{i}", [128, 512], BF16)) for i in range(2)]

        gB = {n: Buf(n) for n in
              ["ident", "tril", "maskd", "maskp", "mt", "bdk", "bdv", "onesc",
               "onesr", "onef", "swb", "bo_b", "w3s_b", "bg_sb", "bgo_sb",
               "wgoT", "w1s", "w2s", "h1s", "h2s", "h1s_e", "h2s_e",
               "cvf0", "cvb0", "cvf1", "cvb1", "hsc", "rows", "hx"]}

        block = E(nc.Block())
        engmap = {"PE": block.tensor, "ACT": block.scalar, "DVE": block.vector,
                  "POOL": block.gpsimd, "SP": block.sync}

        def ld(dst, src, reads=(), writes=(), q="SP"):
            return sch.dma(q, lambda e, d=dst, s=src: e.dma_start(out=d, in_=s),
                           reads=reads, writes=writes)

        ld(ident[:], dt["ident_bf"], writes=[gB["ident"]])
        ld(onesc[:], dt["ones_col_bf"], writes=[gB["onesc"]])
        ld(onef[:], dt["one_f"], writes=[gB["onef"]])
        ld(bg_sb[:], dt["bg"], writes=[gB["bg_sb"]])
        ld(bgo_sb[:], dt["bgo"], writes=[gB["bgo_sb"]])
        ld(wgoT_sb[:], dt["wgoT"], writes=[gB["wgoT"]])
        sch.dve(lambda e: e.tensor_copy(wgoT_bf[:], wgoT_sb[:]),
                reads=[gB["wgoT"]], writes=[gB["wgoT"]])
        ld(s_w1s[:], dt["w1s"], writes=[gB["w1s"]])
        ld(s_w2s[:], dt["w2s"], writes=[gB["w2s"]])
        ld(s_h1s[:], dt["h1s"], writes=[gB["h1s"]])
        ld(s_h2s[:], dt["h2s"], writes=[gB["h2s"]])
        sch.dve(lambda e: e.tensor_scalar(out=h1s_e[:], in0=s_h1s[:],
                                          scalar1=float(1 / np.sqrt(2)),
                                          scalar2=None, op0=ALU.mult),
                reads=[gB["h1s"]], writes=[gB["h1s_e"]])
        sch.dve(lambda e: e.tensor_scalar(out=h2s_e[:], in0=s_h2s[:],
                                          scalar1=float(1 / np.sqrt(2)),
                                          scalar2=None, op0=ALU.mult),
                reads=[gB["h2s"]], writes=[gB["h2s_e"]])

        p0stk = contextlib.ExitStack()

        hsr = p0stk.enter_context(nc.sbuf_tensor("hsr", [1, NLVL], F32))
        hsc2 = p0stk.enter_context(nc.sbuf_tensor("hsc2", [1, NLVL], F32))
        hmx = p0stk.enter_context(nc.sbuf_tensor("hmx", [1, 1], F32))
        swf = p0stk.enter_context(nc.sbuf_tensor("swf", [1, NLVL], F32))
        swbf = p0stk.enter_context(nc.sbuf_tensor("swbf", [1, NLVL], BF16))
        onesr_bf = p0stk.enter_context(nc.sbuf_tensor("onesr_bf", [1, CH], BF16))
        ld(onesr_bf[:], dt["ones_row_bf"], writes=[gB["onesr"]])
        ld(hsr[:], dt["haar_scale"], writes=[gB["hx"]])
        sch.dve(lambda e: e.reduce_max(out=hmx[:], in_=hsr[:], axis=AX.X),
                reads=[gB["hx"]], writes=[gB["hx"]])
        sch.dve(lambda e: e.tensor_tensor(out=hsc2[:], in0=hsr[:],
                                          in1=hmx[:].broadcast_to([1, NLVL]),
                                          op=ALU.subtract),
                reads=[gB["hx"]], writes=[gB["hx"]])
        sch.act(lambda e: e.activation(out=hsc2[:], in_=hsc2[:], func=AF.Exp),
                reads=[gB["hx"]], writes=[gB["hx"]])
        sch.dve(lambda e: e.reduce_sum(out=hmx[:], in_=hsc2[:], axis=AX.X),
                reads=[gB["hx"]], writes=[gB["hx"]])
        sch.dve(lambda e: e.reciprocal(out=hmx[:], in_=hmx[:]),
                reads=[gB["hx"]], writes=[gB["hx"]])
        sch.dve(lambda e: e.tensor_tensor(out=swf[:], in0=hsc2[:],
                                          in1=hmx[:].broadcast_to([1, NLVL]),
                                          op=ALU.mult),
                reads=[gB["hx"]], writes=[gB["hx"]])
        sch.dve(lambda e: e.tensor_copy(swbf[:], swf[:]),
                reads=[gB["hx"]], writes=[gB["hx"]])
        sch.pe(lambda e: e.matmul(pb[0][:, 0:NLVL], onesr_bf[:], swbf[:],
                                  start=True, stop=True),
               reads=[gB["hx"], gB["onesr"]], writes=[pB[0]])
        sch.dve(lambda e: e.tensor_copy(swb[:], pb[0][:, 0:NLVL]),
                reads=[pB[0]], writes=[gB["swb"]])


        def conv_chunks():
            specs = [(dt[w + "T"], 8, DM, wp_scr[w], dB[w], None, w)
                     for w in ("wq", "wkl", "wk", "wv", "wvl")]
            specs += [(dt["wgT"], 16, DM, wg_scr, dB["wg_scr"], None, "wg"),
                     (dt["woT"], 8, DM, wo_scr, dB["wo_scr"], None, "wo"),
                     (dt["w1T"], 8, 4096, w1_scr, dB["w1_scr"], None, "w1"),
                     (dt["w2T"], 8, 4096, w2_scr, dB["w2_scr"], None, "w2"),
                     (dt["h1T"], 32, 4096, h1_scr, dB["h1_scr"], None, "h1"),
                     (dt["h2T"], 32, 4096, h2_scr, dB["h2_scr"], None, "h2"),
                     (dt["w3T"], 32, DM, w3_scr, dB["w3_scr"], None, "w3")]
            specs = specs[:5] + specs[5:]
            for src, k, rl, dscr, dbuf, sc, tag in specs:
                for ic in range(k):
                    parts = max(1, rl // 512)
                    for p_ in range(parts):
                        cl = rl // parts
                        yield (src, ic, p_, cl, dscr, dbuf, sc, tag)

        conv_iter = conv_chunks()
        conv_par = [0]

        def emit_conv(n):
            for _ in range(n):
                try:
                    src, ic, p_, cl, dscr, dbuf, sc, tag = next(conv_iter)
                except StopIteration:
                    return
                pi = conv_par[0] % 2
                conv_par[0] += 1
                cf, cb = cvf[pi], cvb[pi]
                kf, kb_ = gB[f"cvf{pi}"], gB[f"cvb{pi}"]
                ld(cf[:, 0:cl], src[ic * 128:(ic + 1) * 128, p_ * cl:(p_ + 1) * cl],
                   writes=[kf], q="POOL")
                if sc is None:
                    sch.pool(lambda e, cl=cl, cf=cf, cb=cb: e.tensor_copy(
                        cb[:, 0:cl], cf[:, 0:cl]),
                        reads=[kf], writes=[kb_])
                else:
                    sch.pool(lambda e, cl=cl, sc=sc, cf=cf, cb=cb: e.tensor_scalar(
                        out=cb[:, 0:cl], in0=cf[:, 0:cl], scalar1=sc, scalar2=None,
                        op0=ALU.mult),
                        reads=[kf], writes=[kb_])
                sch.dma("POOL", lambda e, dscr=dscr, ic=ic, p_=p_, cl=cl,
                        cb=cb: e.dma_start(
                    out=dscr[ic, :, p_ * cl:(p_ + 1) * cl], in_=cb[:, 0:cl]),
                    reads=[kb_], writes=[(dbuf, (ic, p_))])

        emit_conv(80)
        sch.flush(engmap)
        sch.barrier()
        p0stk.close()
        n_conv_left = 2 * (16 + 8 + 8 * 4 + 8 * 4 + 32 * 4 + 32 * 4 + 32)

        KPH = os.environ.get("KPH", "ABC")
        with contextlib.ExitStack() as pas:
            EA = pas.enter_context
            tril = EA(nc.sbuf_tensor("tril", [CH, CH], BF16))
            maskd = EA(nc.sbuf_tensor("maskd", [CH, CH], F32))
            maskp = EA(nc.sbuf_tensor("maskp", [CH, CH], F32))
            maskp0 = EA(nc.sbuf_tensor("maskp0", [CH, CH], F32))
            mt = EA(nc.sbuf_tensor("mt", [CH, NLVL * CH], BF16))
            bdk = EA(nc.sbuf_tensor("bdk", [128, NLVL * 128], BF16))
            bdv = EA(nc.sbuf_tensor("bdv", [128, NLVL * 128], BF16))
            hscf = EA(nc.sbuf_tensor("hscf", [128, NLVL * 128], F32))
            ld(tril[:], dt["tril_bf"], writes=[gB["tril"]])
            ld(maskd[:], dt["mask_diag"], writes=[gB["maskd"]])
            ld(maskp[:], dt["mask_prev"], writes=[gB["maskp"]])
            ld(maskp0[:], dt["mask_prev0"], writes=[gB["maskp"]])
            ld(mt[:].rearrange("p (l b) -> p l b", b=128),
               dt["mt_bf"].transpose([1, 0, 2]), writes=[gB["mt"]])
            for src_t, tgt, nm in ((dt["haarWkT"], bdk, "bdk"),
                                   (dt["haarWvT"], bdv, "bdv")):
                sch.dve(lambda e: e.memset(hscf[:], 0.0), writes=[gB["hsc"]])
                for lvl in range(NLVL):
                    ld(hscf[0:64, lvl * 128:lvl * 128 + 64], src_t[lvl],
                       writes=[gB["hsc"]])
                    ld(hscf[64:128, lvl * 128 + 64:lvl * 128 + 128], src_t[lvl],
                       writes=[gB["hsc"]])
                sch.act(lambda e, tgt=tgt: e.copy(tgt[:], hscf[:]),
                        reads=[gB["hsc"]], writes=[gB[nm]])
            WR = [EA(nc.sbuf_tensor(f"wring{i}", [128, 8 * DM], BF16))
                  for i in range(2)]
            WRB = [Buf(f"wring{i}") for i in range(2)]
            wr_slot_of = {}

            def load_proj(s, wi, wname):
                slot = (s * 5 + wi) % 2
                wr_slot_of[wname] = slot
                sch.dma("SP", lambda e, wname=wname, slot=slot: e.dma_start(
                    out=WR[slot][:].rearrange("p (i f) -> p i f", f=DM),
                    in_=wp_scr[wname][:, :, :].transpose([1, 0, 2])),
                    reads=[dB[wname]], writes=[WRB[slot]])
            x_t = EA(nc.sbuf_tensor("x_t", [128, 2048], F32))
            stats = EA(nc.sbuf_tensor("stats", [128, 12], F32))
            xn_t = EA(nc.sbuf_tensor("xn_t", [128, 2048], BF16))
            xn_f = EA(nc.sbuf_tensor("xn_f", [128, 2048], BF16))
            q_f = EA(nc.sbuf_tensor("q_f", [128, 2048], BF16))
            k_t = EA(nc.sbuf_tensor("k_t", [128, 2048], BF16))
            kp_f = EA(nc.sbuf_tensor("kp_f", [128, 1024], BF16))
            v_t = EA(nc.sbuf_tensor("v_t", [128, 2080], BF16))
            v_b = EA(nc.sbuf_tensor("v_b", [128, 2048], BF16))
            klr = EA(nc.sbuf_tensor("klr", [128, 3 * 1024], BF16))
            vlr = EA(nc.sbuf_tensor("vlr", [128, 3 * 1040], BF16))
            bm_k2 = [EA(nc.sbuf_tensor(f"bm_k{i}", [128, 4096], BF16)) for i in range(2)]
            bm_v2 = [EA(nc.sbuf_tensor(f"bm_v{i}", [128, 4096], BF16)) for i in range(2)]
            klv2 = [EA(nc.sbuf_tensor(f"klv2_{i}", [128, 2048], BF16)) for i in range(3)]
            vlv = [EA(nc.sbuf_tensor(f"vlv{i}", [128, 1040], BF16)) for i in range(3)]
            ab = [EA(nc.sbuf_tensor(f"ab{i}", [128, 2048], BF16)) for i in range(2)]
            ed = EA(nc.sbuf_tensor("ed", [128, 1024], BF16))
            ep = EA(nc.sbuf_tensor("ep", [128, 1024], BF16))
            scl = EA(nc.sbuf_tensor("scl", [128, 512], F32))
            glob = EA(nc.sbuf_tensor("glob", [128, 2048], F32))
            loc = EA(nc.sbuf_tensor("loc", [128, 2048], BF16))
            glob_bf = EA(nc.sbuf_tensor("glob_bf", [128, 1024], BF16))
            s_all = EA(nc.sbuf_tensor("s_all", [128, 5 * 520], F32))
            sh_kv = EA(nc.sbuf_tensor("sh_kv", [128, 5 * 520], BF16))
            scrP2 = [EA(nc.sbuf_tensor(f"scrP{i}", [128, 2048], BF16)) for i in range(2)]
            scrT = EA(nc.sbuf_tensor("scrT", [128, 512], F32))
            scrF = EA(nc.sbuf_tensor("scrF", [128, 40], F32))

            aB = {n: Buf(n) for n in
                  ["x_t", "junk", "stats", "xn_t", "xn_f", "q_f", "k_t", "kp_f",
                   "v_t", "v_b", "klr", "vlr", "bm_k0", "bm_v0", "bm_k1", "bm_v1",
                   "ab0", "ab1", "ed", "ep",
                   "scl", "glob", "loc", "glob_bf", "s_all", "sh_kv",
                   "scrP0", "scrP1", "scrT", "scrF"]}
            lvB = [{n: Buf(f"{n}{i}") for n in ("klv", "vlv")} for i in range(3)]

            sch.dve(lambda e: e.memset(klr[:, 2048:3072], 0.0), writes=[(aB["klr"], 2)])
            sch.dve(lambda e: e.memset(v_t[:], 1.0), writes=[aB["v_t"]])
            sch.dve(lambda e: e.memset(vlr[:], 1.0), writes=[aB["vlr"]])
            for i in range(3):
                sch.dve(lambda e, i=i: e.memset(vlv[i][:], 1.0), writes=[lvB[i]["vlv"]])
            sch.dve(lambda e: e.memset(s_all[:], 0.0), writes=[aB["s_all"]])
            sch.dve(lambda e: e.memset(sh_kv[:], 0.0), writes=[aB["sh_kv"]])

            def phi_inplace_ap(get_ap, bufB, n, par):
                scrP, sK = scrP2[par], aB[f"scrP{par}"]
                sch.dve(lambda e, scrP=scrP: e.tensor_scalar(
                    out=scrP[:, 0:n], in0=get_ap(), scalar1=0.0,
                    scalar2=None, op0=ALU.min),
                    reads=[bufB], writes=[sK])
                sch.act(lambda e, n=n, scrP=scrP: e.activation(
                    out=scrP[:, 0:n], in_=scrP[:, 0:n], func=AF.Exp),
                    reads=[sK], writes=[sK])
                sch.act(lambda e: e.activation(out=get_ap(), in_=get_ap(),
                                               func=AF.Relu),
                        reads=[bufB], writes=[bufB])
                sch.dve(lambda e, n=n, scrP=scrP: e.tensor_tensor(
                    out=get_ap(), in0=get_ap(),
                    in1=scrP[:, 0:n] if get_ap().ndim == 2 else
                    scrP[:, 0:n].rearrange("p (a b) -> p a b", b=128),
                    op=ALU.add),
                    reads=[bufB, sK], writes=[bufB])

            def phi_inplace(buf, bufB, lo, n, par):
                phi_inplace_ap(lambda buf=buf, lo=lo, n=n: buf[:, lo:lo + n], bufB, n, par)

            def qsl(h, sub):
                return q_f[(h % 2) * 64:(h % 2) * 64 + 64,
                           (h // 2) * 256 + sub * 128:(h // 2) * 256 + sub * 128 + 128]

            def la_call(sub, c, kpf, kpfB, kpt, kptB, kpt_lo, vv, vvB, vv_lo,
                        prod=None):
                KLAC = os.environ.get("KLAC", "aekosf")
                par = c % 2
                abuf, abB = ab[par], aB[f"ab{par}"]
                for w in range(2):
                    pa_ = (pb[2 * w], pb[2 * w + 1])
                    paB_ = (pB[2 * w], pB[2 * w + 1])
                    for j in range(8 if "a" in KLAC else 0):
                        h = w * 8 + j
                        bank, bankB = pa_[j % 2], paB_[j % 2]
                        sch.pe(lambda e, h=h, j=j, bank=bank, kpf=kpf, sub=sub: e.matmul(
                            bank[:, (j // 2) * 128:(j // 2 + 1) * 128],
                            kpf[(h % 2) * 64:(h % 2) * 64 + 64,
                                (h // 2) * 128:(h // 2) * 128 + 128],
                            qsl(h, sub), start=True, stop=True),
                            reads=[kpfB, aB["q_f"]], writes=[bankB])
                    for half in range(2 if "e" in KLAC else 0):
                        bank, bankB = pa_[half], paB_[half]
                        akey = (w, half)
                        sch.dve(lambda e, w=w, half=half, bank=bank, abuf=abuf: e.tensor_tensor(
                            out=abuf[:, w * 1024 + half * 512:w * 1024 + (half + 1) * 512]
                            .rearrange("p (j t) -> p j t", t=128),
                            in0=bank[:, 0:512].rearrange("p (j t) -> p j t", t=128),
                            in1=tril[:].unsqueeze(1).broadcast_to([128, 4, 128]),
                            op=ALU.mult),
                            reads=[bankB, gB["tril"]], writes=[(abB, akey)])
                for w in range(2):
                    for j in range(8 if "s" in KLAC else 0):
                        h = w * 8 + j
                        sb_, sbB_ = pb[2 * w], pB[2 * w]
                        sch.pe(lambda e, h=h, kpt=kpt, kpt_lo=kpt_lo, vv=vv,
                               vv_lo=vv_lo, sb_=sb_: e.matmul(
                            sb_[(h % 2) * 64:(h % 2) * 64 + 64,
                                ((h // 2) % 4) * 65:((h // 2) % 4) * 65 + 65],
                            kpt[:, kpt_lo + h * 64:kpt_lo + h * 64 + 64],
                            vv[:, vv_lo + h * 65:vv_lo + h * 65 + 65],
                            start=True, stop=True),
                            reads=[kptB, vvB], writes=[sbB_])
                for w in range(2):
                    for j in range(8 if "o" in KLAC else 0):
                        h = w * 8 + j
                        akey = (w, j % 2)
                        rh = (h % 2) * 64
                        abo = w * 1024 + (j % 2) * 512 + (j // 2) * 128
                        po_, poB_ = pb[4 + 2 * w + j // 4], pB[4 + 2 * w + j // 4]
                        sch.pe(lambda e, j=j, h=h, po_=po_, vv=vv, vv_lo=vv_lo,
                               abo=abo, abuf=abuf: e.matmul(
                            po_[:, (j % 4) * 65:(j % 4) * 65 + 65],
                            abuf[:, abo:abo + 128],
                            vv[:, vv_lo + h * 65:vv_lo + h * 65 + 65],
                            start=True, stop=False),
                            reads=[(abB, akey), vvB], writes=[poB_])
                        sch.pe(lambda e, h=h, j=j, c=c, rh=rh, po_=po_, sub=sub: e.matmul(
                            po_[:, (j % 4) * 65:(j % 4) * 65 + 65], qsl(h, sub),
                            sh_kv[rh:rh + 64,
                                  c * 520 + (h // 2) * 65:c * 520 + (h // 2) * 65 + 65],
                            start=False, stop=True),
                            reads=[aB["q_f"], aB["sh_kv"]], writes=[poB_])
                if "f" not in KLAC:
                    return
                for g4 in range(4):
                    bank, bankB = pb[4 + g4], pB[4 + g4]
                    sch.dve(lambda e, g4=g4, bank=bank: e.tensor_scalar(
                        out=scrF[:, g4 * 4:(g4 + 1) * 4].unsqueeze(2),
                        in0=bank[:, 0:260].rearrange("p (a b) -> p a b", b=65)[:, :, 64:65],
                        scalar1=EPS_LA, scalar2=None, op0=ALU.max),
                        reads=[bankB], writes=[(aB["scrF"], "c")])
                sch.dve(lambda e: e.reciprocal(out=scrF[:, 0:16], in_=scrF[:, 0:16]),
                        reads=[(aB["scrF"], "c")], writes=[(aB["scrF"], "c")])
                if c > 0:
                    sch.dve(lambda e, c=c: e.tensor_tensor(
                        out=scrF[:, 0:16], in0=scrF[:, 0:16],
                        in1=swb[:, c - 1:c].broadcast_to([128, 16]), op=ALU.mult),
                        reads=[(aB["scrF"], "c"), gB["swb"]], writes=[(aB["scrF"], "c")])
                for g4 in range(4):
                    bank, bankB = pb[4 + g4], pB[4 + g4]
                    gkey = (sub, g4)
                    gsl = glob[:, sub * 1024 + g4 * 256:sub * 1024 + (g4 + 1) * 256]
                    fac = scrF[:, g4 * 4:(g4 + 1) * 4].unsqueeze(2).broadcast_to([128, 4, 64])
                    if c == 0:
                        sch.dve(lambda e, bank=bank, gsl=gsl, fac=fac: e.tensor_tensor(
                            out=gsl.rearrange("p (h z) -> p h z", z=64),
                            in0=bank[:, 0:260].rearrange("p (a b) -> p a b", b=65)[:, :, 0:64],
                            in1=fac, op=ALU.mult),
                            reads=[bankB, (aB["scrF"], "c")], writes=[(aB["glob"], gkey)])
                    else:
                        sch.dve(lambda e, bank=bank, fac=fac: e.tensor_tensor(
                            out=scrT[:, 0:256].rearrange("p (h z) -> p h z", z=64),
                            in0=bank[:, 0:260].rearrange("p (a b) -> p a b", b=65)[:, :, 0:64],
                            in1=fac, op=ALU.mult),
                            reads=[bankB, (aB["scrF"], "c")], writes=[aB["scrT"]])
                        sch.dve(lambda e, gsl=gsl: e.tensor_tensor(
                            out=gsl, in0=gsl, in1=scrT[:, 0:256], op=ALU.add),
                            reads=[aB["scrT"], (aB["glob"], gkey)],
                            writes=[(aB["glob"], gkey)])
                for w in range(2):
                    sch.dve(lambda e, c=c, w=w: e.tensor_tensor(
                        out=s_all[:, c * 520 + w * 260:c * 520 + (w + 1) * 260],
                        in0=s_all[:, c * 520 + w * 260:c * 520 + (w + 1) * 260],
                        in1=pb[2 * w][:, 0:260], op=ALU.add),
                        reads=[pB[2 * w], (aB["s_all"], (c, w))],
                        writes=[(aB["s_all"], (c, w))])
                if prod is not None:
                    prod()

            def chunk_prep(s, sub, stage):
                KCHP = os.environ.get("KCHP", "blpmhs")
                g = 2 * s + sub
                slot, pslot = g % 3, (g - 1) % 3
                if stage == 0:
                    for src, srcB, dst, dstB in (((k_t, aB["k_t"], bm_k2[sub],
                                                   aB[f"bm_k{sub}"]),
                                                 (v_b, aB["v_b"], bm_v2[sub],
                                                  aB[f"bm_v{sub}"]))
                                                 if "b" in KCHP else ()):
                        for dc in range(8):
                            bank, pbk = pB[dc % 4], pb[dc % 4]
                            for lvl in range(NLVL):
                                lh = src[:, sub * 1024 + dc * 128:
                                         sub * 1024 + (dc + 1) * 128]
                                sch.pe(lambda e, lh=lh, lvl=lvl, pbk=pbk: e.matmul(
                                    pbk[:, lvl * 128:(lvl + 1) * 128], lh,
                                    mt[:, lvl * 128:(lvl + 1) * 128],
                                    start=True, stop=True),
                                    reads=[srcB, gB["mt"]], writes=[bank])
                            sch.act(lambda e, dst=dst, dc=dc, pbk=pbk: e.copy(
                                dst[:, dc * 512:(dc + 1) * 512], pbk[:, 0:512]),
                                reads=[bank], writes=[(dstB, dc)])

                elif stage == 1:
                    for w in range(2 if "l" in KCHP else 0):
                        for j in range(8):
                            h = w * 8 + j
                            bank, pbk = pB[j // 4], pb[j // 4]
                            sch.pe(lambda e, pbk=pbk, j=j, h=h, slot=slot, sub=sub: e.matmul(
                                pbk[:, (j % 4) * 128:(j % 4 + 1) * 128],
                                klr[(h % 2) * 64:(h % 2) * 64 + 64,
                                    slot * 1024 + (h // 2) * 128:slot * 1024 + (h // 2) * 128 + 128],
                                qsl(h, sub), start=True, stop=True),
                                reads=[aB["klr"], aB["q_f"]], writes=[bank])
                            pbx, pbxB = (pb[2], pB[2]) if j < 4 else (pb[4], pB[4])
                            sch.pe(lambda e, j=j, h=h, pslot=pslot, sub=sub,
                                   pbx=pbx: e.matmul(
                                pbx[64:128, (j % 4) * 128:(j % 4) * 128 + 128],
                                klr[(h % 2) * 64:(h % 2) * 64 + 64,
                                    pslot * 1024 + (h // 2) * 128 + 64:
                                    pslot * 1024 + (h // 2) * 128 + 128],
                                qsl(h, sub), start=True, stop=True),
                                reads=[aB["klr"], aB["q_f"]], writes=[pbxB])
                        for half in range(2):
                            bank, pbk = pB[half], pb[half]
                            sch.dve(lambda e, pbk=pbk: e.tensor_tensor(
                                out=scl[:, 0:512].rearrange("p (j t) -> p j t", t=128),
                                in0=pbk[:, 0:512].rearrange("p (j t) -> p j t", t=128),
                                in1=maskd[:].unsqueeze(1).broadcast_to([128, 4, 128]),
                                op=ALU.add),
                                reads=[bank, gB["maskd"]], writes=[aB["scl"]])
                            sch.act(lambda e, half=half: e.activation(
                                out=ed[:, half * 512:(half + 1) * 512], in_=scl[:, 0:512],
                                func=AF.Exp, scale=0.125),
                                reads=[aB["scl"]], writes=[(aB["ed"], half)])
                        mp_use = maskp if g > 0 else maskp0
                        for pk in range(2):
                            pbx, pbxB = (pb[2], pB[2]) if pk == 0 else (pb[4], pB[4])
                            sch.dve(lambda e, pbx=pbx, mp_use=mp_use: e.tensor_tensor(
                                out=scl[64:128, 0:512].rearrange("p (j t) -> p j t", t=128),
                                in0=pbx[64:128, 0:512].rearrange("p (j t) -> p j t", t=128),
                                in1=mp_use[64:128, :].unsqueeze(1).broadcast_to([64, 4, 128]),
                                op=ALU.add),
                                reads=[pbxB, gB["maskp"]], writes=[aB["scl"]])
                            sch.act(lambda e, pk=pk: e.activation(
                                out=ep[64:128, pk * 512:(pk + 1) * 512],
                                in_=scl[64:128, 0:512], func=AF.Exp, scale=0.125),
                                reads=[aB["scl"]], writes=[aB["ep"]])
                        for j in range(8):
                            h = w * 8 + j
                            obk, obkB = (pb[3], pB[3]) if j < 4 else (pb[5], pB[5])
                            sch.pe(lambda e, j=j, h=h, slot=slot, obk=obk: e.matmul(
                                obk[:, (j % 4) * 65:(j % 4) * 65 + 65],
                                ed[:, j * 128:(j + 1) * 128],
                                vlr[:, slot * 1040 + h * 65:slot * 1040 + h * 65 + 65],
                                start=True, stop=False),
                                reads=[(aB["ed"], j // 4), aB["vlr"]], writes=[obkB])
                            sch.pe(lambda e, j=j, h=h, pslot=pslot, obk=obk: e.matmul(
                                obk[:, (j % 4) * 65:(j % 4) * 65 + 65],
                                ep[64:128, j * 128:(j + 1) * 128],
                                vlr[64:128, pslot * 1040 + h * 65:pslot * 1040 + h * 65 + 65],
                                start=False, stop=True),
                                reads=[aB["ep"], aB["vlr"]], writes=[obkB])
                        for bk2 in range(2):
                            obk, obkB = (pb[3], pB[3]) if bk2 == 0 else (pb[5], pB[5])
                            sch.dve(lambda e, w=w, bk2=bk2, obk=obk: e.tensor_scalar(
                                out=scrF[:, 16 + w * 8 + bk2 * 4:20 + w * 8 + bk2 * 4]
                                .unsqueeze(2),
                                in0=obk[:, 0:260].rearrange("p (a b) -> p a b", b=65)
                                [:, :, 64:65],
                                scalar1=EPS_LA, scalar2=None, op0=ALU.max),
                                reads=[obkB], writes=[(aB["scrF"], "l")])
                        sch.dve(lambda e, w=w: e.reciprocal(
                            out=scrF[:, 16 + w * 8:24 + w * 8],
                            in_=scrF[:, 16 + w * 8:24 + w * 8]),
                            reads=[(aB["scrF"], "l")], writes=[(aB["scrF"], "l")])
                        for bk2 in range(2):
                            obk, obkB = (pb[3], pB[3]) if bk2 == 0 else (pb[5], pB[5])
                            sch.dve(lambda e, w=w, sub=sub, bk2=bk2, obk=obk: e.tensor_tensor(
                                out=loc[:, sub * 1024 + w * 512 + bk2 * 256:
                                        sub * 1024 + w * 512 + bk2 * 256 + 256]
                                .rearrange("p (h z) -> p h z", z=64),
                                in0=obk[:, 0:260].rearrange("p (a b) -> p a b", b=65)
                                [:, :, 0:64],
                                in1=scrF[:, 16 + w * 8 + bk2 * 4:20 + w * 8 + bk2 * 4]
                                .unsqueeze(2).broadcast_to([128, 4, 64]),
                                op=ALU.mult),
                                reads=[obkB, (aB["scrF"], "l")], writes=[(aB["loc"], sub)])

                elif stage == 2:
                    if "p" in KCHP:
                        phi_inplace_ap(
                            lambda sub=sub: q_f[:].rearrange(
                                "p (a b) -> p a b", b=256)[:, 0:8, sub * 128:sub * 128 + 128],
                            aB["q_f"], 1024, sub)
                        phi_inplace(k_t, aB["k_t"], sub * 1024, 1024, sub)
                    for grp in range(2 if "p" in KCHP else 0):
                        bank, pbk = pB[grp], pb[grp]
                        for j in range(4):
                            dc = grp * 4 + j
                            sch.pe(lambda e, dc=dc, j=j, pbk=pbk, sub=sub: e.matmul(
                                pbk[:, j * 128:(j + 1) * 128],
                                k_t[:, sub * 1024 + dc * 128:sub * 1024 + (dc + 1) * 128],
                                ident[:], start=True, stop=True),
                                reads=[aB["k_t"], gB["ident"]], writes=[bank])
                        sch.act(lambda e, grp=grp, pbk=pbk: e.copy(
                            kp_f[:, grp * 512:(grp + 1) * 512], pbk[:, 0:512]),
                            reads=[bank], writes=[(aB["kp_f"], grp)])


            def chunk_calls(s, sub, nxt=None):
                KCHP = os.environ.get("KCHP", "blpmhs")
                g = 2 * s + sub
                slot, pslot = g % 3, (g - 1) % 3
                def produce_lvl(lvl):
                    par = (4 * g + lvl) % 3
                    lB = lvB[par]
                    for which, wmat, wmatB, tok in (
                            ("kF", bdk, gB["bdk"], False),
                            ("kT", bdk, gB["bdk"], True),
                            ("vT", bdv, gB["bdv"], True)):
                        bm, bmB = ((bm_k2[sub], aB[f"bm_k{sub}"]) if which[0] == "k"
                                   else (bm_v2[sub], aB[f"bm_v{sub}"]))
                        for grp in range(2):
                            bank, pbk = pB[1 + 2 * grp], pb[1 + 2 * grp]
                            for j in range(4):
                                dc = grp * 4 + j
                                if not tok:
                                    sch.pe(lambda e, lvl=lvl, dc=dc, j=j, pbk=pbk,
                                           wmat=wmat, bm=bm: e.matmul(
                                        pbk[:, j * 128:(j + 1) * 128],
                                        wmat[:, lvl * 128:(lvl + 1) * 128],
                                        bm[:, dc * 512 + lvl * 128:dc * 512 + (lvl + 1) * 128],
                                        start=True, stop=True),
                                        reads=[wmatB, (bmB, dc)], writes=[bank])
                                else:
                                    sch.pe(lambda e, lvl=lvl, dc=dc, j=j, pbk=pbk,
                                           wmat=wmat, bm=bm: e.matmul(
                                        pbk[:, j * 128:(j + 1) * 128],
                                        bm[:, dc * 512 + lvl * 128:dc * 512 + (lvl + 1) * 128],
                                        wmat[:, lvl * 128:(lvl + 1) * 128],
                                        start=True, stop=True),
                                        reads=[wmatB, (bmB, dc)], writes=[bank])
                            if which == "vT":
                                sch.act(lambda e, grp=grp, pbk=pbk, par=par: e.copy(
                                    vlv[par][:, grp * 520:(grp + 1) * 520]
                                    .rearrange("p (h z) -> p h z", z=65)[:, :, 0:64],
                                    pbk[:, 0:512].rearrange("p (h z) -> p h z", z=64)),
                                    reads=[bank], writes=[(lB["vlv"], grp)])
                            else:
                                off = (0 if which == "kF" else 1024) + grp * 512
                                sch.act(lambda e, off=off, pbk=pbk, par=par: e.copy(
                                    klv2[par][:, off:off + 512], pbk[:, 0:512]),
                                    reads=[bank], writes=[(lB["klv"], (which, grp))])
                    phi_inplace(klv2[par], lB["klv"], 0, 2048, sub)



                if "m" in KCHP:
                    if "h" in KCHP:
                        produce_lvl(0)
                        produce_lvl(1)
                    la_call(sub, 0, kp_f, aB["kp_f"], k_t, aB["k_t"], sub * 1024,
                            v_t, aB["v_t"], sub * 1040,
                            prod=(lambda: produce_lvl(2)) if "h" in KCHP else None)
                    if nxt is not None:
                        chunk_prep(nxt[0], nxt[1], 0)
                for lvl in range(NLVL if "h" in KCHP else 0):
                    par = (4 * g + lvl) % 3
                    lB = lvB[par]
                    la_call(sub, lvl + 1, klv2[par], lB["klv"],
                            klv2[par], lB["klv"], 1024, vlv[par], lB["vlv"], 0,
                            prod=(lambda lvl=lvl: produce_lvl(lvl + 3))
                            if lvl < NLVL - 3 else None)
                    if nxt is not None and lvl < 2:
                        chunk_prep(nxt[0], nxt[1], lvl + 1)
                if "s" not in KCHP:
                    return
                sch.act(lambda e: e.copy(sh_kv[:], s_all[:]),
                        reads=[aB["s_all"]], writes=[aB["sh_kv"]])

            def prep_ln(s):
                T0 = s * 256
                sch.dma("SP", lambda e, T0=T0: e.dma_start(
                    out=x_t[:].rearrange("p (s f) -> p s f", f=1024),
                    in_=x_d[T0:T0 + 256, :].rearrange("(s p) f -> p s f", p=128)),
                    writes=[aB["x_t"]])
                for sub in range(2):
                    sch.act(lambda e, sub=sub: e.activation(
                        out=scrP2[0][:, 0:1024], in_=x_t[:, sub * 1024:(sub + 1) * 1024],
                        func=AF.Square, accum_out=stats[:, sub:sub + 1]),
                        reads=[aB["x_t"]], writes=[aB["scrP0"], aB["stats"]])
                    sch.act(lambda e, sub=sub: e.activation(
                        out=scrP2[0][:, 0:1024], in_=x_t[:, sub * 1024:(sub + 1) * 1024],
                        func=AF.Copy, accum_out=stats[:, 2 + sub:3 + sub]),
                        reads=[aB["x_t"]], writes=[aB["scrP0"], aB["stats"]])
                sch.dve(lambda e: e.tensor_scalar(
                    out=stats[:, 4:6], in0=stats[:, 2:4], scalar1=-1.0 / DM,
                    scalar2=None, op0=ALU.mult),
                    reads=[aB["stats"]], writes=[aB["stats"]])
                sch.dve(lambda e: e.tensor_tensor(
                    out=stats[:, 6:8], in0=stats[:, 4:6], in1=stats[:, 4:6],
                    op=ALU.mult),
                    reads=[aB["stats"]], writes=[aB["stats"]])
                sch.dve(lambda e: e.tensor_scalar(
                    out=stats[:, 0:2], in0=stats[:, 0:2], scalar1=1.0 / DM,
                    scalar2=None, op0=ALU.mult),
                    reads=[aB["stats"]], writes=[aB["stats"]])
                sch.dve(lambda e: e.tensor_tensor(
                    out=stats[:, 6:8], in0=stats[:, 0:2], in1=stats[:, 6:8],
                    op=ALU.subtract),
                    reads=[aB["stats"]], writes=[aB["stats"]])
                sch.dve(lambda e: e.tensor_scalar(
                    out=stats[:, 6:8], in0=stats[:, 6:8], scalar1=EPS_LN,
                    scalar2=None, op0=ALU.add),
                    reads=[aB["stats"]], writes=[aB["stats"]])
                sch.act(lambda e: e.activation(out=stats[:, 8:10], in_=stats[:, 6:8],
                                               func=AF.Sqrt),
                        reads=[aB["stats"]], writes=[aB["stats"]])
                sch.dve(lambda e: e.reciprocal(out=stats[:, 8:10], in_=stats[:, 8:10]),
                        reads=[aB["stats"]], writes=[aB["stats"]])
                for sub in range(2):
                    sch.dve(lambda e, sub=sub: e.tensor_scalar(
                        out=xn_t[:, sub * 1024:(sub + 1) * 1024],
                        in0=x_t[:, sub * 1024:(sub + 1) * 1024],
                        scalar1=stats[:, 4 + sub:5 + sub],
                        scalar2=stats[:, 8 + sub:9 + sub],
                        op0=ALU.add, op1=ALU.mult),
                        reads=[aB["x_t"], aB["stats"]], writes=[(aB["xn_t"], sub)])

            def prep_xp(s):
                T0 = s * 256
                for grp in range(4):
                    bank, pbk = pB[grp], pb[grp]
                    idx0 = grp * 4
                    sub0, c0 = idx0 // 8, idx0 % 8
                    for j in range(4):
                        idx = idx0 + j
                        sb_, c_ = idx // 8, idx % 8
                        sch.pe(lambda e, pbk=pbk, j=j, sb_=sb_, c_=c_: e.matmul(
                            pbk[:, j * 128:(j + 1) * 128],
                            xn_t[:, sb_ * 1024 + c_ * 128:sb_ * 1024 + (c_ + 1) * 128],
                            ident[:], start=True, stop=True),
                            reads=[(aB["xn_t"], sb_), gB["ident"]], writes=[bank])
                    sch.act(lambda e, pbk=pbk, sub0=sub0, c0=c0: e.copy(
                        xn_f[:].rearrange("p (c t) -> p c t", t=256)
                        [:, c0:c0 + 4, sub0 * 128:(sub0 + 1) * 128],
                        pbk[:].rearrange("p (j t) -> p j t", t=128)),
                        reads=[bank], writes=[(aB["xn_f"], (sub0, c0))])
                sch.dma("ACT", lambda e, T0=T0: e.dma_start(
                    out=xn_scr[:, :, T0:T0 + 256].transpose([1, 0, 2]),
                    in_=xn_f[:].rearrange("p (c t) -> p c t", t=256)),
                    reads=[aB["xn_f"]], writes=[(dB["xn_scr"], s)])

            def phase_a_body(s):
                T0 = s * 256
                worder = ("wq", "wkl", "wk", "wv", "wvl")
                load_proj(s, 0, "wq")
                for wi, (wname, featdst) in enumerate((("wq", "q"), ("wkl", "kl"))):
                    load_proj(s, wi + 1, worder[wi + 1])
                    wsl = wr_slot_of[wname]
                    for half in range(2):
                        for ocg in range(4):
                            oc = half * 4 + ocg
                            for ic in range(8):
                                sch.pe(lambda e, wsl=wsl, oc=oc, ocg=ocg,
                                       ic=ic: e.matmul(
                                    pb[ocg][:, 0:256],
                                    WR[wsl][:, ic * 1024 + oc * 128:
                                            ic * 1024 + (oc + 1) * 128],
                                    xn_f[:, ic * 256:(ic + 1) * 256],
                                    start=(ic == 0), stop=(ic == 7)),
                                    reads=[WRB[wsl], aB["xn_f"]], writes=[pB[ocg]])
                        for ocg in range(4):
                            oc = half * 4 + ocg
                            if featdst == "q":
                                sch.act(lambda e, oc=oc, ocg=ocg: e.copy(
                                    q_f[:, oc * 256:(oc + 1) * 256], pb[ocg][:, 0:256]),
                                    reads=[pB[ocg]], writes=[aB["q_f"]])
                            else:
                                for sub in range(2):
                                    slot = (2 * s + sub) % 3
                                    sch.act(lambda e, oc=oc, ocg=ocg, sub=sub,
                                            slot=slot: e.copy(
                                        klr[:, slot * 1024 + oc * 128:
                                            slot * 1024 + (oc + 1) * 128],
                                        pb[ocg][:, sub * 128:(sub + 1) * 128]),
                                        reads=[pB[ocg]], writes=[aB["klr"]])
                for wi, (wname, dest, destB, ring) in enumerate((
                        ("wk", k_t, aB["k_t"], False),
                        ("wv", v_t, aB["v_t"], False),
                        ("wvl", vlr, aB["vlr"], True))):
                    if 3 + wi < 5:
                        load_proj(s, 3 + wi, worder[3 + wi])
                    wsl = wr_slot_of[wname]
                    stride65 = wname in ("wv", "wvl")
                    for sub in range(2):
                        slot = (2 * s + sub) % 3
                        if stride65:
                            base = slot * 1040 if ring else sub * 1040
                        else:
                            base = slot * 1024 if ring else sub * 1024
                        for half in range(2):
                            bank, pbk = pB[4 + half], pb[4 + half]
                            for ic in range(8):
                                sch.pe(lambda e, wsl=wsl, sub=sub, half=half,
                                       ic=ic, pbk=pbk: e.matmul(
                                    pbk[:, 0:512],
                                    xn_f[:, ic * 256 + sub * 128:
                                         ic * 256 + sub * 128 + 128],
                                    WR[wsl][:, ic * 1024 + half * 512:
                                            ic * 1024 + half * 512 + 512],
                                    start=(ic == 0), stop=(ic == 7)),
                                    reads=[WRB[wsl], aB["xn_f"]], writes=[bank])
                            if stride65:
                                sch.act(lambda e, dest=dest, base=base, half=half,
                                        pbk=pbk: e.copy(
                                    dest[:, base + half * 520:base + half * 520 + 520]
                                    .rearrange("p (h z) -> p h z", z=65)[:, :, 0:64],
                                    pbk[:, 0:512].rearrange("p (h z) -> p h z", z=64)),
                                    reads=[bank], writes=[destB])
                                if wname == "wv":
                                    sch.act(lambda e, sub=sub, half=half, pbk=pbk: e.copy(
                                        v_b[:, sub * 1024 + half * 512:
                                            sub * 1024 + half * 512 + 512],
                                        pbk[:, 0:512]),
                                        reads=[bank], writes=[aB["v_b"]])
                            else:
                                sch.act(lambda e, dest=dest, base=base, half=half,
                                        pbk=pbk: e.copy(
                                    dest[:, base + half * 512:base + half * 512 + 512],
                                    pbk[:, 0:512]),
                                    reads=[bank], writes=[destB])
                if s + 1 < nA:
                    prep_ln(s + 1)
                for st in range(3):
                    chunk_prep(s, 0, st)
                chunk_calls(s, 0, nxt=(s, 1))
                if s + 1 < nA:
                    prep_xp(s + 1)
                chunk_calls(s, 1, nxt=None)
                for sub in range(2):
                    sch.act(lambda e, sub=sub: e.copy(
                        glob_bf[:],
                        glob[:, sub * 1024:(sub + 1) * 1024]),
                        reads=[(aB["glob"], (sub, 0)), (aB["glob"], (sub, 1))],
                        writes=[aB["glob_bf"]])
                    sch.dma("ACT", lambda e, T0=T0, sub=sub: e.dma_start(
                        out=glob_scr[T0 + sub * 128:T0 + (sub + 1) * 128, :],
                        in_=glob_bf[:]),
                        reads=[aB["glob_bf"]], writes=[(dB["glob_scr"], (s, sub))])
                sch.dma("ACT", lambda e, T0=T0: e.dma_start(
                    out=local_scr[T0:T0 + 256, :].rearrange("(s p) f -> p s f", p=128),
                    in_=loc[:].rearrange("p (s f) -> p s f", f=1024)),
                    reads=[(aB["loc"], 0), (aB["loc"], 1)],
                    writes=[(dB["local_scr"], s)])

            conv_per = max(1, (n_conv_left + NSUPA - 1) // NSUPA)
            nA = NSUPA if "A" in KPH else 0
            nA = min(nA, int(os.environ.get("KSUPA", "999")))
            if nA > 0:
                prep_ln(0)
                prep_xp(0)
            for s in range(nA):
                phase_a_body(s)
                emit_conv(conv_per)
                sch.flush(engmap)
            emit_conv(10000)
            sch.flush(engmap)
            sch.barrier()

        with contextlib.ExitStack() as pbs:
            EB = pbs.enter_context
            wg_sb = EB(nc.sbuf_tensor("wg_sb", [128, 16 * DM], BF16))
            wo_sb = EB(nc.sbuf_tensor("wo_sb", [128, 8 * DM], BF16))
            xnf_b = EB(nc.sbuf_tensor("xnf_b", [128, 8 * 512], BF16))
            gl_b = EB(nc.sbuf_tensor("gl_b", [128, 4 * 1024], BF16))
            lo_b = EB(nc.sbuf_tensor("lo_b", [128, 4 * 1024], BF16))
            df_f = EB(nc.sbuf_tensor("df_f", [128, 8 * 512], BF16))
            gt_f = EB(nc.sbuf_tensor("gt_f", [128, 8 * 512], BF16))
            mx_f = EB(nc.sbuf_tensor("mx_f", [128, 8 * 512], BF16))
            xb_t = EB(nc.sbuf_tensor("xb_t", [128, 4 * 1024], F32))
            al_r = EB(nc.sbuf_tensor("al_r", [1, 512], F32))
            al_c = EB(nc.sbuf_tensor("al_c", [128, 4], F32))
            sco = EB(nc.sbuf_tensor("sco", [128, 1024], F32))
            scm = EB(nc.sbuf_tensor("scm", [128, 1024], BF16))
            outs = EB(nc.sbuf_tensor("outs", [128, 1024], F32))
            bo_b = EB(nc.sbuf_tensor("bo_b", [128, DM], F32))

            bB = {n: Buf(n) for n in ["wg_sb", "wo_sb", "xnf_b", "gl_b", "lo_b",
                                      "df_f", "gt_f", "mx_f", "xb_t", "al_r",
                                      "al_c", "sco", "scm", "outs"]}

            ld(bo_b[:], dt["bo_bc"], writes=[gB["bo_b"]])
            sch.dma("SP", lambda e: e.dma_start(
                out=wg_sb[:].rearrange("p (i f) -> p i f", f=DM),
                in_=wg_scr[:, :, :].transpose([1, 0, 2])),
                reads=[dB["wg_scr"]], writes=[bB["wg_sb"]])
            sch.dma("SP", lambda e: e.dma_start(
                out=wo_sb[:].rearrange("p (i f) -> p i f", f=DM),
                in_=wo_scr[:, :, :].transpose([1, 0, 2])),
                reads=[dB["wo_scr"]], writes=[bB["wo_sb"]])

            def phase_b_supertile(sb):
                T0 = sb * 512
                sch.dma("SP", lambda e, T0=T0: e.dma_start(
                    out=gl_b[:].rearrange("p (s f) -> p s f", f=1024),
                    in_=glob_scr[T0:T0 + 512, :]
                    .rearrange("(s p) f -> p s f", p=128)),
                    reads=[dB["glob_scr"]], writes=[bB["gl_b"]])
                sch.dma("SP", lambda e, T0=T0: e.dma_start(
                    out=lo_b[:].rearrange("p (s f) -> p s f", f=1024),
                    in_=local_scr[T0:T0 + 512, :]
                    .rearrange("(s p) f -> p s f", p=128)),
                    reads=[dB["local_scr"]], writes=[bB["lo_b"]])
                sch.dma("SP", lambda e, T0=T0: e.dma_start(
                    out=xnf_b[:].rearrange("p (c t) -> p c t", t=512),
                    in_=xn_scr[:, :, T0:T0 + 512].transpose([1, 0, 2])),
                    reads=[dB["xn_scr"]], writes=[bB["xnf_b"]])
                sch.dma("SP", lambda e, T0=T0: e.dma_start(
                    out=xb_t[:].rearrange("p (s f) -> p s f", f=1024),
                    in_=x_d[T0:T0 + 512, :]
                    .rearrange("(s p) f -> p s f", p=128)),
                    writes=[bB["xb_t"]])
                sch.dve(lambda e: e.tensor_tensor(out=lo_b[:], in0=lo_b[:],
                                                  in1=gl_b[:], op=ALU.subtract),
                        reads=[bB["gl_b"], bB["lo_b"]], writes=[bB["lo_b"]])
                for c in range(8):
                    bank, pbk = pB[c % 4], pb[c % 4]
                    for sq in range(4):
                        sch.pe(lambda e, c=c, sq=sq, pbk=pbk: e.matmul(
                            pbk[:, sq * 128:(sq + 1) * 128],
                            lo_b[:, sq * 1024 + c * 128:sq * 1024 + (c + 1) * 128],
                            ident[:], start=True, stop=True),
                            reads=[bB["lo_b"], gB["ident"]], writes=[bank])
                    sch.act(lambda e, c=c, pbk=pbk: e.copy(
                        df_f[:, c * 512:(c + 1) * 512], pbk[:, 0:512]),
                        reads=[bank], writes=[(bB["df_f"], c)])
                for oc in range(8):
                    bank, pbk = pB[4 + oc % 2], pb[4 + oc % 2]
                    for ic in range(8):
                        sch.pe(lambda e, oc=oc, ic=ic, pbk=pbk: e.matmul(
                            pbk[:, 0:512],
                            wg_sb[:, ic * 1024 + oc * 128:ic * 1024 + (oc + 1) * 128],
                            xnf_b[:, ic * 512:(ic + 1) * 512],
                            start=(ic == 0), stop=False),
                            reads=[bB["wg_sb"], bB["xnf_b"]], writes=[bank])
                    for ic in range(8):
                        sch.pe(lambda e, oc=oc, ic=ic, pbk=pbk: e.matmul(
                            pbk[:, 0:512],
                            wg_sb[:, (8 + ic) * 1024 + oc * 128:
                                  (8 + ic) * 1024 + (oc + 1) * 128],
                            df_f[:, ic * 512:(ic + 1) * 512],
                            start=False, stop=(ic == 7)),
                            reads=[bB["wg_sb"], (bB["df_f"], ic)], writes=[bank])
                    sch.act(lambda e, oc=oc, pbk=pbk: e.activation(
                        out=gt_f[:, oc * 512:(oc + 1) * 512], in_=pbk[:, 0:512],
                        func=AF.Silu, bias=bg_sb[:, oc:oc + 1], scale=1.0),
                        reads=[bank, gB["bg_sb"]], writes=[(bB["gt_f"], oc)])
                for oc in range(8):
                    sch.pe(lambda e, oc=oc: e.matmul(
                        pb[6][0:1, 0:512], wgoT_bf[:, oc:oc + 1],
                        gt_f[:, oc * 512:(oc + 1) * 512],
                        start=(oc == 0), stop=(oc == 7)),
                        reads=[(bB["gt_f"], oc), gB["wgoT"]], writes=[pB[6]])
                sch.act(lambda e: e.activation(out=al_r[:], in_=pb[6][0:1, 0:512],
                                               func=AF.Sigmoid, bias=bgo_sb[:],
                                               scale=1.0),
                        reads=[pB[6], gB["bgo_sb"]], writes=[bB["al_r"]])
                for sq in range(4):
                    sch.pe(lambda e, sq=sq: e.matmul(
                        pb[7][:, sq:sq + 1], al_r[0:1, sq * 128:(sq + 1) * 128],
                        onef[:], start=True, stop=True),
                        reads=[bB["al_r"], gB["onef"]], writes=[pB[7]])
                sch.dve(lambda e: e.tensor_copy(al_c[:], pb[7][:, 0:4]),
                        reads=[pB[7]], writes=[bB["al_c"]])
                for sq in range(4):
                    sch.dve(lambda e, sq=sq: e.tensor_scalar(
                        out=scm[:], in0=lo_b[:, sq * 1024:(sq + 1) * 1024],
                        scalar1=al_c[:, sq:sq + 1], scalar2=None, op0=ALU.mult),
                        reads=[bB["lo_b"], bB["al_c"]], writes=[bB["scm"]])
                    sch.dve(lambda e, sq=sq: e.tensor_tensor(
                        out=gl_b[:, sq * 1024:(sq + 1) * 1024],
                        in0=gl_b[:, sq * 1024:(sq + 1) * 1024], in1=scm[:],
                        op=ALU.add),
                        reads=[bB["scm"], bB["gl_b"]], writes=[bB["gl_b"]])
                for c in range(8):
                    bank, pbk = pB[c % 4], pb[c % 4]
                    for sq in range(4):
                        sch.pe(lambda e, c=c, sq=sq, pbk=pbk: e.matmul(
                            pbk[:, sq * 128:(sq + 1) * 128],
                            gl_b[:, sq * 1024 + c * 128:sq * 1024 + (c + 1) * 128],
                            ident[:], start=True, stop=True),
                            reads=[bB["gl_b"], gB["ident"]], writes=[bank])
                    sch.act(lambda e, c=c, pbk=pbk: e.copy(
                        mx_f[:, c * 512:(c + 1) * 512], pbk[:, 0:512]),
                        reads=[bank], writes=[(bB["mx_f"], c)])
                for sq in range(4):
                    for half in range(2):
                        bank, pbk = pB[4 + half], pb[4 + half]
                        for ic in range(8):
                            sch.pe(lambda e, sq=sq, half=half, ic=ic, pbk=pbk: e.matmul(
                                pbk[:, 0:512],
                                mx_f[:, ic * 512 + sq * 128:ic * 512 + sq * 128 + 128],
                                wo_sb[:, ic * 1024 + half * 512:
                                      ic * 1024 + half * 512 + 512],
                                start=(ic == 0), stop=(ic == 7)),
                                reads=[(bB["mx_f"], ic), bB["wo_sb"]], writes=[bank])
                        sch.dve(lambda e, sq=sq, half=half, pbk=pbk: e.tensor_tensor(
                            out=sco[:, half * 512:(half + 1) * 512], in0=pbk[:, 0:512],
                            in1=xb_t[:, sq * 1024 + half * 512:
                                     sq * 1024 + half * 512 + 512],
                            op=ALU.add),
                            reads=[bank, bB["xb_t"]], writes=[(bB["sco"], half)])
                        sch.dve(lambda e, half=half: e.tensor_tensor(
                            out=outs[:, half * 512:(half + 1) * 512],
                            in0=sco[:, half * 512:(half + 1) * 512],
                            in1=bo_b[:, half * 512:(half + 1) * 512], op=ALU.add),
                            reads=[(bB["sco"], half), gB["bo_b"]],
                            writes=[(bB["outs"], half)])
                    sch.dma("ACT", lambda e, T0=T0, sq=sq: e.dma_start(
                        out=out_d[T0 + sq * 128:T0 + (sq + 1) * 128, :], in_=outs[:]),
                        reads=[(bB["outs"], 0), (bB["outs"], 1)],
                        writes=[(dB["out"], (sb, sq))])

            for sb in range(NSUPB if "B" in KPH else 0):
                phase_b_supertile(sb)
                sch.flush(engmap)
            sch.flush(engmap)
            sch.barrier()

        with contextlib.ExitStack() as pcs:
            EC = pcs.enter_context
            h_a = EC(nc.sbuf_tensor("h_a", [128, 32 * CT], BF16))
            h_b = EC(nc.sbuf_tensor("h_b", [128, 32 * CT], BF16))
            xnf_c = h_b
            wA = [EC(nc.sbuf_tensor(f"wAa{i}", [128, 8 * 512], BF16)) for i in range(2)]
            wb2 = [EC(nc.sbuf_tensor(f"wBb{i}", [128, 8 * 512], BF16)) for i in range(2)]
            wH = [EC(nc.sbuf_tensor(f"wHh{i}", [128, 32 * 256], BF16)) for i in range(2)]
            w3sb = wH
            scs = EC(nc.sbuf_tensor("scs", [128, 512], BF16))
            sct = EC(nc.sbuf_tensor("sct", [128, 512], BF16))
            outc = EC(nc.sbuf_tensor("outc", [128, 256], F32))
            w3sc = EC(nc.sbuf_tensor("w3sc", [128, 256], F32))

            cB = {n: Buf(n) for n in ["h_a", "h_b", "scs", "sct", "outc"]}
            cB["xnf_c"] = cB["h_b"]
            wAB = [Buf(f"wAa{i}") for i in range(2)]
            wBB = [Buf(f"wBb{i}") for i in range(2)]
            wHB = [Buf(f"wHh{i}") for i in range(2)]
            w3B = wHB

            def phase_c_tile(tc):
                T0 = tc * CT
                NTH = CT // 512
                sch.dma("SP", lambda e, T0=T0: e.dma_start(
                    out=xnf_c[:, 0:8 * CT].rearrange("p (c t) -> p c t", t=CT),
                    in_=xn_scr[:, :, T0:T0 + CT].transpose([1, 0, 2])),
                    reads=[dB["xn_scr"]], writes=[cB["xnf_c"]])
                for og in range(8):
                    par = og % 2
                    sch.dma("SP", lambda e, og=og, par=par: e.dma_start(
                        out=wA[par][:].rearrange("p (i f) -> p i f", f=512),
                        in_=w1_scr[:, :, og * 512:(og + 1) * 512].transpose([1, 0, 2])),
                        reads=[dB["w1_scr"]], writes=[wAB[par]])
                    sch.dma("SP", lambda e, og=og, par=par: e.dma_start(
                        out=wb2[par][:].rearrange("p (i f) -> p i f", f=512),
                        in_=w2_scr[:, :, og * 512:(og + 1) * 512].transpose([1, 0, 2])),
                        reads=[dB["w2_scr"]], writes=[wBB[par]])
                    for oci in range(4):
                        oc = og * 4 + oci
                        for th in range(NTH):
                            b1, p1 = pB[2 * (oci % 2)], pb[2 * (oci % 2)]
                            b2, p2 = pB[2 * (oci % 2) + 1], pb[2 * (oci % 2) + 1]
                            for ic in range(8):
                                sch.pe(lambda e, par=par, oci=oci, ic=ic, th=th,
                                       p1=p1: e.matmul(
                                    p1[:, 0:512],
                                    wA[par][:, ic * 512 + oci * 128:
                                            ic * 512 + (oci + 1) * 128],
                                    xnf_c[:, ic * CT + th * 512:ic * CT + th * 512 + 512],
                                    start=(ic == 0), stop=(ic == 7)),
                                    reads=[wAB[par], cB["xnf_c"]], writes=[b1])
                            for ic in range(8):
                                sch.pe(lambda e, par=par, oci=oci, ic=ic, th=th,
                                       p2=p2: e.matmul(
                                    p2[:, 0:512],
                                    wb2[par][:, ic * 512 + oci * 128:
                                             ic * 512 + (oci + 1) * 128],
                                    xnf_c[:, ic * CT + th * 512:ic * CT + th * 512 + 512],
                                    start=(ic == 0), stop=(ic == 7)),
                                    reads=[wBB[par], cB["xnf_c"]], writes=[b2])
                            sch.act(lambda e, oc=oc, p1=p1: e.activation(
                                out=scs[:], in_=p1[:, 0:512], func=AF.Silu,
                                scale=s_w1s[:, oc:oc + 1]),
                                reads=[b1, gB["w1s"]], writes=[cB["scs"]])
                            sch.dve(lambda e, oc=oc, p2=p2: e.tensor_scalar(
                                out=sct[:], in0=p2[:, 0:512],
                                scalar1=s_w2s[:, oc:oc + 1], scalar2=None, op0=ALU.mult),
                                reads=[b2, gB["w2s"]], writes=[cB["sct"]])
                            sch.dve(lambda e, oc=oc, th=th: e.tensor_tensor(
                                out=h_a[:, oc * CT + th * 512:oc * CT + th * 512 + 512],
                                in0=scs[:], in1=sct[:], op=ALU.mult),
                                reads=[cB["scs"], cB["sct"]],
                                writes=[(cB["h_a"], (oc, th))])

                def hstage(wscr, wscrB, src, srcB, dst, dstB, s_col, sB1, s_e, sB2):
                    for og in range(16):
                        par = og % 2
                        sch.dma("SP", lambda e, og=og, par=par, wscr=wscr: e.dma_start(
                            out=wH[par][:].rearrange("p (i f) -> p i f", f=256),
                            in_=wscr[:, :, og * 256:(og + 1) * 256].transpose([1, 0, 2])),
                            reads=[wscrB], writes=[wHB[par]])
                        for oci in range(2):
                            oc = og * 2 + oci
                            for th in range(NTH):
                                bk, pk = pB[2 * oci + th % 2], pb[2 * oci + th % 2]
                                for ic in range(32):
                                    sch.pe(lambda e, par=par, oci=oci, ic=ic, th=th,
                                           pk=pk, src=src: e.matmul(
                                        pk[:, 0:512],
                                        wH[par][:, ic * 256 + oci * 128:
                                                ic * 256 + (oci + 1) * 128],
                                        src[:, ic * CT + th * 512:
                                            ic * CT + th * 512 + 512],
                                        start=(ic == 0), stop=(ic == 31)),
                                        reads=[wHB[par], srcB], writes=[bk])
                                sch.act(lambda e, oc=oc, th=th, pk=pk, dst=dst,
                                        s_col=s_col: e.activation(
                                    out=dst[:, oc * CT + th * 512:
                                            oc * CT + th * 512 + 512],
                                    in_=pk[:, 0:512], func=AF.Gelu,
                                    scale=s_col[:, oc:oc + 1]),
                                    reads=[bk, sB1], writes=[(dstB, (oc, th))])

                hstage(h1_scr, dB["h1_scr"], h_a, cB["h_a"], h_b, cB["h_b"],
                       s_h1s, gB["h1s"], h1s_e, gB["h1s_e"])
                hstage(h2_scr, dB["h2_scr"], h_b, cB["h_b"], h_a, cB["h_a"],
                       s_h2s, gB["h2s"], h2s_e, gB["h2s_e"])

                nsub = CT // 128
                for og in range(4):
                    par = og % 2
                    sch.dma("SP", lambda e, og=og, par=par: e.dma_start(
                        out=w3sb[par][:].rearrange("p (i f) -> p i f", f=256),
                        in_=w3_scr[:, :, og * 256:(og + 1) * 256].transpose([1, 0, 2])),
                        reads=[dB["w3_scr"]], writes=[w3B[par]])
                    sch.dma("SP", lambda e, og=og: e.dma_start(
                        out=w3sc[:], in_=dt["w3s_bc"][:, og * 256:(og + 1) * 256]),
                        writes=[gB["w3s_b"]])
                    for sq in range(nsub):
                        okey = (T0 // 512 + sq // 4, sq % 4)
                        bk, pk = pB[4 + sq % 2], pb[4 + sq % 2]
                        for ic in range(32):
                            sch.pe(lambda e, par=par, sq=sq, ic=ic, pk=pk: e.matmul(
                                pk[:, 0:256],
                                h_a[:, ic * CT + sq * 128:ic * CT + (sq + 1) * 128],
                                w3sb[par][:, ic * 256:(ic + 1) * 256],
                                start=(ic == 0), stop=(ic == 31)),
                                reads=[w3B[par], cB["h_a"]], writes=[bk])
                        sch.dve(lambda e, pk=pk: e.tensor_tensor(
                            out=outc[:], in0=pk[:, 0:256],
                            in1=w3sc[:], op=ALU.mult),
                            reads=[bk, gB["w3s_b"]], writes=[cB["outc"]])
                        sch.dma("POOL", lambda e, T0=T0, og=og, sq=sq: e.dma_start(
                            out=out_d[T0 + sq * 128:T0 + (sq + 1) * 128,
                                      og * 256:(og + 1) * 256],
                            in_=outc[:], accum_op=ALU.add),
                            reads=[cB["outc"]], writes=[(dB["out"], okey)])
                sch.flush(engmap)

            for tc in range(NSUPC if "C" in KPH else 0):
                phase_c_tile(tc)
            sch.flush(engmap)
            sch.barrier()
            sch.pool(lambda e: e.tensor_copy(cvb[0][:, 0:1], cvb[0][:, 0:1]))
            sch.flush(engmap)

    return nc



def make_inputs(full, core, N=4096, shared=None):
    if shared is not None:
        m = dict(shared)
        m["x"] = np.ascontiguousarray(np.asarray(full["L0_attn_x"][core], np.float32)[:N])
        return m
    T = lambda a: np.ascontiguousarray(np.asarray(a, np.float32).T)
    c = host_consts()
    m = {
        "x": np.ascontiguousarray(np.asarray(full["L0_attn_x"][core], np.float32)[:N]),
        "wqT": T(full["L0_attn_Wq"]), "wkT": T(full["L0_attn_Wk"]),
        "wvT": T(full["L0_attn_Wv"]), "wklT": T(full["L0_attn_Wkl"]),
        "wvlT": T(full["L0_attn_Wvl"]), "woT": T(full["L0_attn_Wo"]),
        "wgT": T(full["L0_attn_Wg"]),
        "wgoT": np.ascontiguousarray(
            np.asarray(full["L0_attn_Wgo"], np.float32).reshape(8, 128).T),
        "haarWkT": np.ascontiguousarray(
            np.transpose(np.asarray(full["L0_attn_haar_Wk"], np.float32), (0, 2, 1))),
        "haarWvT": np.ascontiguousarray(
            np.transpose(np.asarray(full["L0_attn_haar_Wv"], np.float32), (0, 2, 1))),
        "haar_scale": np.asarray(full["L0_attn_haar_scale"], np.float32).reshape(1, 4),
        "bg": np.ascontiguousarray(
            np.asarray(full["L0_attn_bg"], np.float32).reshape(8, 128).T),
        "bgo": np.asarray(full["L0_attn_bgo"], np.float32).reshape(1, 1),
        "bo_row": np.asarray(full["L0_attn_bo"], np.float32).reshape(1, DM),
        "w1T": T(full["L0_ffn_w1_w"]), "w2T": T(full["L0_ffn_w2_w"]),
        "h1T": T(full["L0_ffn_h1_w"]), "h2T": T(full["L0_ffn_h2_w"]),
        "w3T": T(full["L0_ffn_w3_w"]),
        "w1s": np.ascontiguousarray(
            np.asarray(full["L0_ffn_w1_s"], np.float32).reshape(32, 128).T),
        "w2s": np.ascontiguousarray(
            np.asarray(full["L0_ffn_w2_s"], np.float32).reshape(32, 128).T),
        "h1s": np.ascontiguousarray(
            np.asarray(full["L0_ffn_h1_s"], np.float32).reshape(32, 128).T),
        "h2s": np.ascontiguousarray(
            np.asarray(full["L0_ffn_h2_s"], np.float32).reshape(32, 128).T),
        "w3s_row": np.asarray(full["L0_ffn_w3_s"], np.float32).reshape(1, DM),
        "w3s_bc": np.ascontiguousarray(np.broadcast_to(
            np.asarray(full["L0_ffn_w3_s"], np.float32).reshape(1, DM), (128, DM))),
        "bo_bc": np.ascontiguousarray(np.broadcast_to(
            np.asarray(full["L0_attn_bo"], np.float32).reshape(1, DM), (128, DM))),
    }
    m.update({k: np.asarray(v) for k, v in c.items()})
    return m


def kernel(**inputs):
    N = 4096
    nc = build_program(N)
    shared = make_inputs(inputs, 0, N)
    in_maps = [make_inputs(inputs, core, N, shared=shared) for core in range(NCORES)]
    res = run_bass_kernel_spmd(nc, in_maps, core_ids=list(range(NCORES)))
    return np.stack([np.asarray(r["out"], np.float32) for r in res.results])


# version 16
# speedup vs baseline: 1.1978x; 1.1978x over previous
import contextlib
import os
import numpy as np
import ml_dtypes

import concourse.bass as bass
import concourse.mybir as mybir
from concourse.bass_utils import run_bass_kernel_spmd

F32 = mybir.dt.float32
BF16 = mybir.dt.bfloat16
AF = mybir.ActivationFunctionType
ALU = mybir.AluOpType
AX = mybir.AxisListType

NCORES = 8
DM = 1024
H = 16
DH = 64
CH = 128
NLVL = 4
EPS_LN = 1e-5
EPS_LA = 1e-6
NEG = -1.0e30
BF = ml_dtypes.bfloat16



ENGINES = ("PE", "ACT", "DVE", "POOL", "SP")


class Op:
    __slots__ = ("engine", "kind", "fn", "deps", "marked", "seq", "idx", "slot")

    def __init__(self, engine, kind, fn, deps, idx):
        self.engine, self.kind, self.fn = engine, kind, fn
        self.deps, self.idx = deps, idx
        self.marked = False
        self.seq = None
        self.slot = None


class Buf:
    __slots__ = ("name", "writers", "readers")

    def __init__(self, name):
        self.name = name
        self.writers = {}
        self.readers = {}

    def _conf(self, table, key):
        out = []
        if key is None:
            for ops in table.values():
                out.extend(ops)
        else:
            out.extend(table.get(key, ()))
            out.extend(table.get(None, ()))
        return out

    def dep_read(self, key):
        return self._conf(self.writers, key)

    def dep_write(self, key):
        return self._conf(self.writers, key) + self._conf(self.readers, key)

    def note_read(self, key, op):
        self.readers.setdefault(key, []).append(op)

    def note_write(self, key, op):
        if key is None:
            self.writers.clear()
            self.readers.clear()
        else:
            self.writers.pop(key, None)
            self.readers.pop(key, None)
        self.writers.setdefault(key, []).append(op)


def _norm(items):
    return [it if isinstance(it, tuple) else (it, None) for it in items]


class Sched:
    def __init__(self, nc):
        self.nc = nc
        self.ops = []
        self.pending = []
        self.barrier_deps = []
        self.cnt = {e: 0 for e in ENGINES}
        self.dma_cnt = {e: 0 for e in ENGINES}
        self.dma_use = {}
        self.dma_last = {}
        self.sems = {}
        self.dma_sems = {}
        self.hw = {e: {} for e in ENGINES}
        self.flushed_upto = 0
        self.flushed_last = {}
        self.NSLOT = 4

    def init_sems(self, stk):
        for e in ENGINES:
            self.sems[e] = stk.enter_context(self.nc.semaphore(f"sem_{e}"))
            for sl in range(self.NSLOT):
                self.dma_sems[(e, sl)] = stk.enter_context(
                    self.nc.semaphore(f"dsem_{e}{sl}"))
                self.dma_use[(e, sl)] = 0

    def _fix(self, d):
        if d.idx < self.flushed_upto and not d.marked:
            key = (d.engine, d.kind) if d.kind == "c" else (d.engine, "d", d.slot)
            return self.flushed_last[key]
        return d

    def add(self, engine, fn, reads=(), writes=(), kind="c", deps=()):
        out, seen = [], set()

        def push(d):
            d = self._fix(d)
            if id(d) not in seen:
                seen.add(id(d))
                out.append(d)

        for b, k in _norm(reads):
            for d in b.dep_read(k):
                push(d)
        for b, k in _norm(writes):
            for d in b.dep_write(k):
                push(d)
        for d in deps:
            push(d)
        for d in self.barrier_deps:
            push(d)
        op = Op(engine, kind, fn, out, len(self.ops))
        self.ops.append(op)
        self.pending.append(op)
        for b, k in _norm(reads):
            b.note_read(k, op)
        for b, k in _norm(writes):
            b.note_write(k, op)
        return op

    def pe(self, fn, reads=(), writes=(), deps=()):
        return self.add("PE", fn, reads, writes, "c", deps)

    def act(self, fn, reads=(), writes=(), deps=()):
        return self.add("ACT", fn, reads, writes, "c", deps)

    def dve(self, fn, reads=(), writes=(), deps=()):
        return self.add("DVE", fn, reads, writes, "c", deps)

    def pool(self, fn, reads=(), writes=(), deps=()):
        return self.add("POOL", fn, reads, writes, "c", deps)

    def dma(self, q, fn, reads=(), writes=(), deps=()):
        return self.add(q, fn, reads, writes, "d", deps)

    def barrier(self):
        assert not self.pending, "barrier must follow a flush"
        last = {}
        for op in self.ops:
            key = (op.engine, "c") if op.kind == "c" else (op.engine, "d", op.slot)
            last[key] = op
        self.barrier_deps = list(last.values())

    def flush(self, engmap):
        pend, self.pending = self.pending, []
        if not pend:
            return
        for op in pend:
            if op.kind == "d":
                op.marked = True
            best = {}
            dd = []
            for d in op.deps:
                if d.kind == "c":
                    k = d.engine
                    if k not in best or best[k].idx < d.idx:
                        best[k] = d
                else:
                    dd.append(d)
            op.deps = dd + list(best.values())
            for d in op.deps:
                d.marked = True
        last = {}
        for op in pend:
            last[(op.engine, op.kind)] = op
        for k, op in last.items():
            op.marked = True
        for op in pend:
            if op.marked:
                if op.kind == "d":
                    sl = self.dma_cnt[op.engine] % self.NSLOT
                    self.dma_cnt[op.engine] += 1
                    op.slot = sl
                    self.dma_use[(op.engine, sl)] += 1
                    op.seq = 16 * self.dma_use[(op.engine, sl)]
                    prev = self.dma_last.get((op.engine, sl))
                    if prev is not None and prev not in op.deps:
                        op.deps.append(prev)
                    self.dma_last[(op.engine, sl)] = op
                else:
                    self.cnt[op.engine] += 1
                    op.seq = self.cnt[op.engine]
        self.flushed_upto = len(self.ops)
        for op in pend:
            key = (op.engine, "c") if op.kind == "c" else (op.engine, "d", op.slot)
            if op.marked:
                self.flushed_last[key] = op
        for ename in ENGINES:
            eops = [op for op in pend if op.engine == ename]
            if not eops:
                continue

            def run(eng, eops=eops, ename=ename):
                hw = self.hw[ename]
                for op in eops:
                    need = {}
                    for d in op.deps:
                        if d.seq is None:
                            raise RuntimeError("unmarked dep")
                        if d.kind == "d":
                            key = ("d", d.engine, d.slot)
                        else:
                            if (d.engine == ename == "PE" and op.kind == "c"):
                                continue
                            key = ("c", d.engine)
                        if need.get(key, 0) < d.seq:
                            need[key] = d.seq
                    for key, seq in need.items():
                        if hw.get(key, 0) < seq:
                            if key[0] == "d":
                                eng.wait_ge(self.dma_sems[(key[1], key[2])], seq)
                            else:
                                eng.wait_ge(self.sems[key[1]], seq)
                            hw[key] = seq
                    inst = op.fn(eng)
                    if op.marked:
                        if op.kind == "d":
                            inst.then_inc(self.dma_sems[(ename, op.slot)], 16)
                        else:
                            inst.then_inc(self.sems[ename], 1)

            engmap[ename](run)



def host_consts():
    c = {}
    c["ident_bf"] = np.eye(CH, dtype=BF)
    c["tril_bf"] = (np.arange(CH)[:, None] <= np.arange(CH)[None, :]).astype(BF)
    tk = np.arange(CH)[:, None]
    tq = np.arange(CH)[None, :]
    c["mask_diag"] = np.where((tk <= tq) & (tk >= tq - 63), 0.0, NEG).astype(np.float32)
    tkp = np.arange(64)[:, None]
    mp = np.where(tkp >= tq + 1, 0.0, NEG).astype(np.float32)
    c["mask_prev"] = np.concatenate([mp, mp], axis=0)
    c["mask_prev0"] = np.full((CH, CH), NEG, dtype=np.float32)
    mts = []
    for lvl in range(NLVL):
        blk = 2 ** (lvl + 1)
        t_ = np.arange(CH)
        start = (t_ // blk) * blk
        M = np.zeros((CH, CH), dtype=np.float64)
        for tp in range(CH):
            M[tp, start[tp]:tp + 1] = 1.0 / (tp - start[tp] + 1)
        mts.append(M.T.astype(BF))
    c["mt_bf"] = np.ascontiguousarray(np.stack(mts))
    c["ones_col_bf"] = np.ones((CH, 1), dtype=BF)
    c["ones_row_f"] = np.ones((1, CH), dtype=np.float32)
    c["ones_row_bf"] = np.ones((1, CH), dtype=BF)
    c["one_f"] = np.ones((1, 1), dtype=np.float32)
    return c



def build_program(N=4096):
    nc = bass.Bass("TRN2", target_bir_lowering=False)
    NSUPA = N // 256
    NSUPB = N // 512
    CT = 1024 if N >= 1024 else N
    NSUPC = N // CT

    dt = {}

    def din(name, shape, dtype=F32):
        dt[name] = nc.dram_tensor(name, list(shape), dtype, kind="ExternalInput").ap()
        return dt[name]

    x_d = din("x", (N, DM))
    for w in ("wqT", "wkT", "wvT", "wklT", "wvlT", "woT"):
        din(w, (DM, DM))
    din("wgT", (2 * DM, DM))
    din("wgoT", (128, 8))
    din("haarWkT", (NLVL, DH, DH))
    din("haarWvT", (NLVL, DH, DH))
    din("haar_scale", (1, NLVL))
    din("bg", (128, 8))
    din("bgo", (1, 1))
    din("bo_row", (1, DM))
    din("w1T", (DM, 4096))
    din("w2T", (DM, 4096))
    din("h1T", (4096, 4096))
    din("h2T", (4096, 4096))
    din("w3T", (4096, DM))
    din("w1s", (128, 32))
    din("w2s", (128, 32))
    din("h1s", (128, 32))
    din("h2s", (128, 32))
    din("w3s_row", (1, DM))
    din("w3s_bc", (128, DM))
    din("bo_bc", (128, DM))
    din("ident_bf", (CH, CH), BF16)
    din("tril_bf", (CH, CH), BF16)
    din("mask_diag", (CH, CH))
    din("mask_prev", (CH, CH))
    din("mask_prev0", (CH, CH))
    din("mt_bf", (NLVL, CH, CH), BF16)
    din("ones_col_bf", (CH, 1), BF16)
    din("ones_row_f", (1, CH))
    din("ones_row_bf", (1, CH), BF16)
    din("one_f", (1, 1))

    out_d = nc.dram_tensor("out", [N, DM], F32, kind="ExternalOutput").ap()

    xn_scr = nc.dram_tensor("xn_scr", [8, 128, N], BF16, kind="Internal").ap()
    glob_scr = nc.dram_tensor("glob_scr", [N, DM], BF16, kind="Internal").ap()
    local_scr = nc.dram_tensor("local_scr", [N, DM], BF16, kind="Internal").ap()
    wp_scr = {w: nc.dram_tensor(f"wp_{w}", [8, 128, DM], BF16, kind="Internal").ap()
              for w in ("wq", "wk", "wv", "wkl", "wvl")}
    wg_scr = nc.dram_tensor("wg_scr", [16, 128, DM], BF16, kind="Internal").ap()
    wo_scr = nc.dram_tensor("wo_scr", [8, 128, DM], BF16, kind="Internal").ap()
    w1_scr = nc.dram_tensor("w1_scr", [8, 128, 4096], BF16, kind="Internal").ap()
    w2_scr = nc.dram_tensor("w2_scr", [8, 128, 4096], BF16, kind="Internal").ap()
    h1_scr = nc.dram_tensor("h1_scr", [32, 128, 4096], BF16, kind="Internal").ap()
    h2_scr = nc.dram_tensor("h2_scr", [32, 128, 4096], BF16, kind="Internal").ap()
    w3_scr = nc.dram_tensor("w3_scr", [32, 128, DM], BF16, kind="Internal").ap()

    dB = {n: Buf("d_" + n) for n in
          ["xn_scr", "glob_scr", "local_scr", "wg_scr", "wo_scr",
           "w1_scr", "w2_scr", "h1_scr", "h2_scr", "w3_scr", "out",
           "wq", "wk", "wv", "wkl", "wvl"]}

    with contextlib.ExitStack() as top:
        E = top.enter_context
        sch = Sched(nc)
        sch.init_sems(top)

        pb = [E(nc.psum_tensor(f"pb{i}", [128, 512], F32)) for i in range(8)]
        pB = [Buf(f"pb{i}") for i in range(8)]

        ident = E(nc.sbuf_tensor("ident", [CH, CH], BF16))
        onesc = E(nc.sbuf_tensor("onesc", [CH, 1], BF16))
        onef = E(nc.sbuf_tensor("onef", [1, 1], F32))
        swb = E(nc.sbuf_tensor("swb", [128, NLVL], F32))
        bg_sb = E(nc.sbuf_tensor("bg_sb", [128, 8], F32))
        bgo_sb = E(nc.sbuf_tensor("bgo_sb", [1, 1], F32))
        wgoT_sb = E(nc.sbuf_tensor("wgoT_sb", [128, 8], F32))
        wgoT_bf = E(nc.sbuf_tensor("wgoT_bf", [128, 8], BF16))
        s_w1s = E(nc.sbuf_tensor("s_w1s", [128, 32], F32))
        s_w2s = E(nc.sbuf_tensor("s_w2s", [128, 32], F32))
        s_h1s = E(nc.sbuf_tensor("s_h1s", [128, 32], F32))
        s_h2s = E(nc.sbuf_tensor("s_h2s", [128, 32], F32))
        h1s_e = E(nc.sbuf_tensor("h1s_e", [128, 32], F32))
        h2s_e = E(nc.sbuf_tensor("h2s_e", [128, 32], F32))
        cvf = [E(nc.sbuf_tensor(f"cvf{i}", [128, 512], F32)) for i in range(2)]
        cvb = [E(nc.sbuf_tensor(f"cvb{i}", [128, 512], BF16)) for i in range(2)]

        gB = {n: Buf(n) for n in
              ["ident", "tril", "maskd", "maskp", "mt", "bdk", "bdv", "onesc",
               "onesr", "onef", "swb", "bo_b", "w3s_b", "bg_sb", "bgo_sb",
               "wgoT", "w1s", "w2s", "h1s", "h2s", "h1s_e", "h2s_e",
               "cvf0", "cvb0", "cvf1", "cvb1", "hsc", "rows", "hx"]}

        block = E(nc.Block())
        engmap = {"PE": block.tensor, "ACT": block.scalar, "DVE": block.vector,
                  "POOL": block.gpsimd, "SP": block.sync}

        def ld(dst, src, reads=(), writes=(), q="SP"):
            return sch.dma(q, lambda e, d=dst, s=src: e.dma_start(out=d, in_=s),
                           reads=reads, writes=writes)

        ld(ident[:], dt["ident_bf"], writes=[gB["ident"]])
        ld(onesc[:], dt["ones_col_bf"], writes=[gB["onesc"]])
        ld(onef[:], dt["one_f"], writes=[gB["onef"]])
        ld(bg_sb[:], dt["bg"], writes=[gB["bg_sb"]])
        ld(bgo_sb[:], dt["bgo"], writes=[gB["bgo_sb"]])
        ld(wgoT_sb[:], dt["wgoT"], writes=[gB["wgoT"]])
        sch.dve(lambda e: e.tensor_copy(wgoT_bf[:], wgoT_sb[:]),
                reads=[gB["wgoT"]], writes=[gB["wgoT"]])
        ld(s_w1s[:], dt["w1s"], writes=[gB["w1s"]])
        ld(s_w2s[:], dt["w2s"], writes=[gB["w2s"]])
        ld(s_h1s[:], dt["h1s"], writes=[gB["h1s"]])
        ld(s_h2s[:], dt["h2s"], writes=[gB["h2s"]])
        sch.dve(lambda e: e.tensor_scalar(out=h1s_e[:], in0=s_h1s[:],
                                          scalar1=float(1 / np.sqrt(2)),
                                          scalar2=None, op0=ALU.mult),
                reads=[gB["h1s"]], writes=[gB["h1s_e"]])
        sch.dve(lambda e: e.tensor_scalar(out=h2s_e[:], in0=s_h2s[:],
                                          scalar1=float(1 / np.sqrt(2)),
                                          scalar2=None, op0=ALU.mult),
                reads=[gB["h2s"]], writes=[gB["h2s_e"]])

        p0stk = contextlib.ExitStack()

        hsr = p0stk.enter_context(nc.sbuf_tensor("hsr", [1, NLVL], F32))
        hsc2 = p0stk.enter_context(nc.sbuf_tensor("hsc2", [1, NLVL], F32))
        hmx = p0stk.enter_context(nc.sbuf_tensor("hmx", [1, 1], F32))
        swf = p0stk.enter_context(nc.sbuf_tensor("swf", [1, NLVL], F32))
        swbf = p0stk.enter_context(nc.sbuf_tensor("swbf", [1, NLVL], BF16))
        onesr_bf = p0stk.enter_context(nc.sbuf_tensor("onesr_bf", [1, CH], BF16))
        ld(onesr_bf[:], dt["ones_row_bf"], writes=[gB["onesr"]])
        ld(hsr[:], dt["haar_scale"], writes=[gB["hx"]])
        sch.dve(lambda e: e.reduce_max(out=hmx[:], in_=hsr[:], axis=AX.X),
                reads=[gB["hx"]], writes=[gB["hx"]])
        sch.dve(lambda e: e.tensor_tensor(out=hsc2[:], in0=hsr[:],
                                          in1=hmx[:].broadcast_to([1, NLVL]),
                                          op=ALU.subtract),
                reads=[gB["hx"]], writes=[gB["hx"]])
        sch.act(lambda e: e.activation(out=hsc2[:], in_=hsc2[:], func=AF.Exp),
                reads=[gB["hx"]], writes=[gB["hx"]])
        sch.dve(lambda e: e.reduce_sum(out=hmx[:], in_=hsc2[:], axis=AX.X),
                reads=[gB["hx"]], writes=[gB["hx"]])
        sch.dve(lambda e: e.reciprocal(out=hmx[:], in_=hmx[:]),
                reads=[gB["hx"]], writes=[gB["hx"]])
        sch.dve(lambda e: e.tensor_tensor(out=swf[:], in0=hsc2[:],
                                          in1=hmx[:].broadcast_to([1, NLVL]),
                                          op=ALU.mult),
                reads=[gB["hx"]], writes=[gB["hx"]])
        sch.dve(lambda e: e.tensor_copy(swbf[:], swf[:]),
                reads=[gB["hx"]], writes=[gB["hx"]])
        sch.pe(lambda e: e.matmul(pb[0][:, 0:NLVL], onesr_bf[:], swbf[:],
                                  start=True, stop=True),
               reads=[gB["hx"], gB["onesr"]], writes=[pB[0]])
        sch.dve(lambda e: e.tensor_copy(swb[:], pb[0][:, 0:NLVL]),
                reads=[pB[0]], writes=[gB["swb"]])


        def conv_chunks():
            specs = [(dt[w + "T"], 8, DM, wp_scr[w], dB[w], None, w)
                     for w in ("wq", "wkl", "wk", "wv", "wvl")]
            specs += [(dt["wgT"], 16, DM, wg_scr, dB["wg_scr"], None, "wg"),
                     (dt["woT"], 8, DM, wo_scr, dB["wo_scr"], None, "wo"),
                     (dt["w1T"], 8, 4096, w1_scr, dB["w1_scr"], None, "w1"),
                     (dt["w2T"], 8, 4096, w2_scr, dB["w2_scr"], None, "w2"),
                     (dt["h1T"], 32, 4096, h1_scr, dB["h1_scr"], None, "h1"),
                     (dt["h2T"], 32, 4096, h2_scr, dB["h2_scr"], None, "h2"),
                     (dt["w3T"], 32, DM, w3_scr, dB["w3_scr"], None, "w3")]
            specs = specs[:5] + specs[5:]
            for src, k, rl, dscr, dbuf, sc, tag in specs:
                for ic in range(k):
                    parts = max(1, rl // 512)
                    for p_ in range(parts):
                        cl = rl // parts
                        yield (src, ic, p_, cl, dscr, dbuf, sc, tag)

        conv_iter = conv_chunks()
        conv_par = [0]

        def emit_conv(n):
            for _ in range(n):
                try:
                    src, ic, p_, cl, dscr, dbuf, sc, tag = next(conv_iter)
                except StopIteration:
                    return
                pi = conv_par[0] % 2
                conv_par[0] += 1
                cf, cb = cvf[pi], cvb[pi]
                kf, kb_ = gB[f"cvf{pi}"], gB[f"cvb{pi}"]
                ld(cf[:, 0:cl], src[ic * 128:(ic + 1) * 128, p_ * cl:(p_ + 1) * cl],
                   writes=[kf], q="POOL")
                if sc is None:
                    sch.pool(lambda e, cl=cl, cf=cf, cb=cb: e.tensor_copy(
                        cb[:, 0:cl], cf[:, 0:cl]),
                        reads=[kf], writes=[kb_])
                else:
                    sch.pool(lambda e, cl=cl, sc=sc, cf=cf, cb=cb: e.tensor_scalar(
                        out=cb[:, 0:cl], in0=cf[:, 0:cl], scalar1=sc, scalar2=None,
                        op0=ALU.mult),
                        reads=[kf], writes=[kb_])
                sch.dma("POOL", lambda e, dscr=dscr, ic=ic, p_=p_, cl=cl,
                        cb=cb: e.dma_start(
                    out=dscr[ic, :, p_ * cl:(p_ + 1) * cl], in_=cb[:, 0:cl]),
                    reads=[kb_], writes=[(dbuf, (ic, p_))])

        emit_conv(80)
        sch.flush(engmap)
        sch.barrier()
        p0stk.close()
        n_conv_left = 2 * (16 + 8 + 8 * 4 + 8 * 4 + 32 * 4 + 32 * 4 + 32)

        KPH = os.environ.get("KPH", "ABC")
        with contextlib.ExitStack() as pas:
            EA = pas.enter_context
            tril = EA(nc.sbuf_tensor("tril", [CH, CH], BF16))
            maskd = EA(nc.sbuf_tensor("maskd", [CH, CH], F32))
            maskp = EA(nc.sbuf_tensor("maskp", [CH, CH], F32))
            maskp0 = EA(nc.sbuf_tensor("maskp0", [CH, CH], F32))
            mt = EA(nc.sbuf_tensor("mt", [CH, NLVL * CH], BF16))
            bdk = EA(nc.sbuf_tensor("bdk", [128, NLVL * 128], BF16))
            bdv = EA(nc.sbuf_tensor("bdv", [128, NLVL * 128], BF16))
            hscf = EA(nc.sbuf_tensor("hscf", [128, NLVL * 128], F32))
            ld(tril[:], dt["tril_bf"], writes=[gB["tril"]])
            ld(maskd[:], dt["mask_diag"], writes=[gB["maskd"]])
            ld(maskp[:], dt["mask_prev"], writes=[gB["maskp"]])
            ld(maskp0[:], dt["mask_prev0"], writes=[gB["maskp"]])
            ld(mt[:].rearrange("p (l b) -> p l b", b=128),
               dt["mt_bf"].transpose([1, 0, 2]), writes=[gB["mt"]])
            for src_t, tgt, nm in ((dt["haarWkT"], bdk, "bdk"),
                                   (dt["haarWvT"], bdv, "bdv")):
                sch.dve(lambda e: e.memset(hscf[:], 0.0), writes=[gB["hsc"]])
                for lvl in range(NLVL):
                    ld(hscf[0:64, lvl * 128:lvl * 128 + 64], src_t[lvl],
                       writes=[gB["hsc"]])
                    ld(hscf[64:128, lvl * 128 + 64:lvl * 128 + 128], src_t[lvl],
                       writes=[gB["hsc"]])
                sch.act(lambda e, tgt=tgt: e.copy(tgt[:], hscf[:]),
                        reads=[gB["hsc"]], writes=[gB[nm]])
            WR = [EA(nc.sbuf_tensor(f"wring{i}", [128, 8 * DM], BF16))
                  for i in range(2)]
            WRB = [Buf(f"wring{i}") for i in range(2)]
            wr_slot_of = {}

            def load_proj(s, wi, wname):
                slot = (s * 5 + wi) % 2
                wr_slot_of[wname] = slot
                sch.dma("SP", lambda e, wname=wname, slot=slot: e.dma_start(
                    out=WR[slot][:].rearrange("p (i f) -> p i f", f=DM),
                    in_=wp_scr[wname][:, :, :].transpose([1, 0, 2])),
                    reads=[dB[wname]], writes=[WRB[slot]])
            x_t = EA(nc.sbuf_tensor("x_t", [128, 2048], F32))
            stats = EA(nc.sbuf_tensor("stats", [128, 12], F32))
            xn_t = EA(nc.sbuf_tensor("xn_t", [128, 2048], BF16))
            xn_f = EA(nc.sbuf_tensor("xn_f", [128, 2048], BF16))
            q_f = EA(nc.sbuf_tensor("q_f", [128, 3 * 1024], BF16))
            k_t = EA(nc.sbuf_tensor("k_t", [128, 3 * 1024], BF16))
            kp_f = EA(nc.sbuf_tensor("kp_f", [128, 2 * 1024], BF16))
            v_t = EA(nc.sbuf_tensor("v_t", [128, 3 * 1040], BF16))
            v_b = EA(nc.sbuf_tensor("v_b", [128, 3 * 1024], BF16))
            klr = EA(nc.sbuf_tensor("klr", [128, 3 * 1024], BF16))
            vlr = EA(nc.sbuf_tensor("vlr", [128, 3 * 1040], BF16))
            bm_k2 = [EA(nc.sbuf_tensor(f"bm_k{i}", [128, 4096], BF16)) for i in range(2)]
            bm_v2 = [EA(nc.sbuf_tensor(f"bm_v{i}", [128, 4096], BF16)) for i in range(2)]
            klv2 = [EA(nc.sbuf_tensor(f"klv2_{i}", [128, 2048], BF16)) for i in range(3)]
            vlv = [EA(nc.sbuf_tensor(f"vlv{i}", [128, 1040], BF16)) for i in range(3)]
            ab = [EA(nc.sbuf_tensor(f"ab{i}", [128, 2048], BF16)) for i in range(2)]
            ed = EA(nc.sbuf_tensor("ed", [128, 1024], BF16))
            ep = EA(nc.sbuf_tensor("ep", [128, 1024], BF16))
            scl = EA(nc.sbuf_tensor("scl", [128, 512], F32))
            glob = EA(nc.sbuf_tensor("glob", [128, 2048], F32))
            loc = EA(nc.sbuf_tensor("loc", [128, 2048], BF16))
            glob_bf = EA(nc.sbuf_tensor("glob_bf", [128, 1024], BF16))
            s_all = EA(nc.sbuf_tensor("s_all", [128, 5 * 520], F32))
            sh_kv = EA(nc.sbuf_tensor("sh_kv", [128, 5 * 520], BF16))
            scrP2 = [EA(nc.sbuf_tensor(f"scrP{i}", [128, 2048], BF16)) for i in range(2)]
            scrT = EA(nc.sbuf_tensor("scrT", [128, 512], F32))
            scrF = EA(nc.sbuf_tensor("scrF", [128, 40], F32))

            aB = {n: Buf(n) for n in
                  ["x_t", "junk", "stats", "xn_t", "xn_f", "q_f", "k_t", "kp_f",
                   "v_t", "v_b", "klr", "vlr", "bm_k0", "bm_v0", "bm_k1", "bm_v1",
                   "ab0", "ab1", "ed", "ep",
                   "scl", "glob", "loc", "glob_bf", "s_all", "sh_kv",
                   "scrP0", "scrP1", "scrT", "scrF"]}
            lvB = [{n: Buf(f"{n}{i}") for n in ("klv", "vlv")} for i in range(3)]

            sch.dve(lambda e: e.memset(klr[:, 2048:3072], 0.0), writes=[(aB["klr"], 2)])
            sch.dve(lambda e: e.memset(v_t[:], 1.0), writes=[aB["v_t"]])
            sch.dve(lambda e: e.memset(vlr[:], 1.0), writes=[aB["vlr"]]) if False else None
            sch.dve(lambda e: e.memset(vlr[:], 1.0), writes=[aB["vlr"]])
            for i in range(3):
                sch.dve(lambda e, i=i: e.memset(vlv[i][:], 1.0), writes=[lvB[i]["vlv"]])
            sch.dve(lambda e: e.memset(s_all[:], 0.0), writes=[aB["s_all"]])
            sch.dve(lambda e: e.memset(sh_kv[:], 0.0), writes=[aB["sh_kv"]])

            def phi_inplace_ap(get_ap, bufB, n, par):
                scrP, sK = scrP2[par], aB[f"scrP{par}"]
                sch.dve(lambda e, scrP=scrP: e.tensor_scalar(
                    out=scrP[:, 0:n], in0=get_ap(), scalar1=0.0,
                    scalar2=None, op0=ALU.min),
                    reads=[bufB], writes=[sK])
                sch.act(lambda e, n=n, scrP=scrP: e.activation(
                    out=scrP[:, 0:n], in_=scrP[:, 0:n], func=AF.Exp),
                    reads=[sK], writes=[sK])
                sch.act(lambda e: e.activation(out=get_ap(), in_=get_ap(),
                                               func=AF.Relu),
                        reads=[bufB], writes=[bufB])
                sch.dve(lambda e, n=n, scrP=scrP: e.tensor_tensor(
                    out=get_ap(), in0=get_ap(),
                    in1=scrP[:, 0:n] if get_ap().ndim == 2 else
                    scrP[:, 0:n].rearrange("p (a b) -> p a b", b=128),
                    op=ALU.add),
                    reads=[bufB, sK], writes=[bufB])

            def phi_inplace(buf, bufB, lo, n, par):
                phi_inplace_ap(lambda buf=buf, lo=lo, n=n: buf[:, lo:lo + n], bufB, n, par)

            def qsl(h, g):
                slot = g % 3
                return q_f[(h % 2) * 64:(h % 2) * 64 + 64,
                           slot * 1024 + (h // 2) * 128:slot * 1024 + (h // 2) * 128 + 128]

            def la_call(g, c, kpf, kpfB, kpf_lo, kpt, kptB, kpt_lo, vv, vvB, vv_lo,
                        prod=None):
                sub = g % 2
                """linear-attention call c; vv is 65-stride token-major with a
                ones column (head h data at vv_lo + h*65, ones at +64)."""
                KLAC = os.environ.get("KLAC", "aekosf")
                par = c % 2
                abuf, abB = ab[par], aB[f"ab{par}"]
                for w in range(2):
                    pa_ = (pb[2 * w], pb[2 * w + 1])
                    paB_ = (pB[2 * w], pB[2 * w + 1])
                    for j in range(8 if "a" in KLAC else 0):
                        h = w * 8 + j
                        bank, bankB = pa_[j % 2], paB_[j % 2]
                        sch.pe(lambda e, h=h, j=j, bank=bank, kpf=kpf,
                               kpf_lo=kpf_lo, g=g: e.matmul(
                            bank[:, (j // 2) * 128:(j // 2 + 1) * 128],
                            kpf[(h % 2) * 64:(h % 2) * 64 + 64,
                                kpf_lo + (h // 2) * 128:kpf_lo + (h // 2) * 128 + 128],
                            qsl(h, g), start=True, stop=True),
                            reads=[kpfB, aB["q_f"]], writes=[bankB])
                    for half in range(2 if "e" in KLAC else 0):
                        bank, bankB = pa_[half], paB_[half]
                        akey = (w, half)
                        sch.dve(lambda e, w=w, half=half, bank=bank, abuf=abuf: e.tensor_tensor(
                            out=abuf[:, w * 1024 + half * 512:w * 1024 + (half + 1) * 512]
                            .rearrange("p (j t) -> p j t", t=128),
                            in0=bank[:, 0:512].rearrange("p (j t) -> p j t", t=128),
                            in1=tril[:].unsqueeze(1).broadcast_to([128, 4, 128]),
                            op=ALU.mult),
                            reads=[bankB, gB["tril"]], writes=[(abB, akey)])
                for w in range(2):
                    for j in range(8 if "s" in KLAC else 0):
                        h = w * 8 + j
                        sb_, sbB_ = pb[2 * w], pB[2 * w]
                        sch.pe(lambda e, h=h, kpt=kpt, kpt_lo=kpt_lo, vv=vv,
                               vv_lo=vv_lo, sb_=sb_: e.matmul(
                            sb_[(h % 2) * 64:(h % 2) * 64 + 64,
                                ((h // 2) % 4) * 65:((h // 2) % 4) * 65 + 65],
                            kpt[:, kpt_lo + h * 64:kpt_lo + h * 64 + 64],
                            vv[:, vv_lo + h * 65:vv_lo + h * 65 + 65],
                            start=True, stop=True),
                            reads=[kptB, vvB], writes=[sbB_])
                for w in range(2):
                    for j in range(8 if "o" in KLAC else 0):
                        h = w * 8 + j
                        akey = (w, j % 2)
                        rh = (h % 2) * 64
                        abo = w * 1024 + (j % 2) * 512 + (j // 2) * 128
                        po_, poB_ = pb[4 + 2 * w + j // 4], pB[4 + 2 * w + j // 4]
                        sch.pe(lambda e, j=j, h=h, po_=po_, vv=vv, vv_lo=vv_lo,
                               abo=abo, abuf=abuf: e.matmul(
                            po_[:, (j % 4) * 65:(j % 4) * 65 + 65],
                            abuf[:, abo:abo + 128],
                            vv[:, vv_lo + h * 65:vv_lo + h * 65 + 65],
                            start=True, stop=False),
                            reads=[(abB, akey), vvB], writes=[poB_])
                        sch.pe(lambda e, h=h, j=j, c=c, rh=rh, po_=po_, g=g: e.matmul(
                            po_[:, (j % 4) * 65:(j % 4) * 65 + 65], qsl(h, g),
                            sh_kv[rh:rh + 64,
                                  c * 520 + (h // 2) * 65:c * 520 + (h // 2) * 65 + 65],
                            start=False, stop=True),
                            reads=[aB["q_f"], aB["sh_kv"]], writes=[poB_])
                if "f" not in KLAC:
                    return
                for g4 in range(4):
                    bank, bankB = pb[4 + g4], pB[4 + g4]
                    sch.dve(lambda e, g4=g4, bank=bank: e.tensor_scalar(
                        out=scrF[:, g4 * 4:(g4 + 1) * 4].unsqueeze(2),
                        in0=bank[:, 0:260].rearrange("p (a b) -> p a b", b=65)[:, :, 64:65],
                        scalar1=EPS_LA, scalar2=None, op0=ALU.max),
                        reads=[bankB], writes=[(aB["scrF"], "c")])
                sch.dve(lambda e: e.reciprocal(out=scrF[:, 0:16], in_=scrF[:, 0:16]),
                        reads=[(aB["scrF"], "c")], writes=[(aB["scrF"], "c")])
                if c > 0:
                    sch.dve(lambda e, c=c: e.tensor_tensor(
                        out=scrF[:, 0:16], in0=scrF[:, 0:16],
                        in1=swb[:, c - 1:c].broadcast_to([128, 16]), op=ALU.mult),
                        reads=[(aB["scrF"], "c"), gB["swb"]], writes=[(aB["scrF"], "c")])
                for g4 in range(4):
                    bank, bankB = pb[4 + g4], pB[4 + g4]
                    gkey = (sub, g4)
                    gsl = glob[:, sub * 1024 + g4 * 256:sub * 1024 + (g4 + 1) * 256]
                    fac = scrF[:, g4 * 4:(g4 + 1) * 4].unsqueeze(2).broadcast_to([128, 4, 64])
                    if c == 0:
                        sch.dve(lambda e, bank=bank, gsl=gsl, fac=fac: e.tensor_tensor(
                            out=gsl.rearrange("p (h z) -> p h z", z=64),
                            in0=bank[:, 0:260].rearrange("p (a b) -> p a b", b=65)[:, :, 0:64],
                            in1=fac, op=ALU.mult),
                            reads=[bankB, (aB["scrF"], "c")], writes=[(aB["glob"], gkey)])
                    else:
                        sch.dve(lambda e, bank=bank, fac=fac: e.tensor_tensor(
                            out=scrT[:, 0:256].rearrange("p (h z) -> p h z", z=64),
                            in0=bank[:, 0:260].rearrange("p (a b) -> p a b", b=65)[:, :, 0:64],
                            in1=fac, op=ALU.mult),
                            reads=[bankB, (aB["scrF"], "c")], writes=[aB["scrT"]])
                        sch.dve(lambda e, gsl=gsl: e.tensor_tensor(
                            out=gsl, in0=gsl, in1=scrT[:, 0:256], op=ALU.add),
                            reads=[aB["scrT"], (aB["glob"], gkey)],
                            writes=[(aB["glob"], gkey)])
                for w in range(2):
                    sch.dve(lambda e, c=c, w=w: e.tensor_tensor(
                        out=s_all[:, c * 520 + w * 260:c * 520 + (w + 1) * 260],
                        in0=s_all[:, c * 520 + w * 260:c * 520 + (w + 1) * 260],
                        in1=pb[2 * w][:, 0:260], op=ALU.add),
                        reads=[pB[2 * w], (aB["s_all"], (c, w))],
                        writes=[(aB["s_all"], (c, w))])
                if prod is not None:
                    prod()

            def chunk_prep(s, sub, stage):
                KCHP = os.environ.get("KCHP", "blpmhs")
                g = 2 * s + sub
                slot, pslot = g % 3, (g - 1) % 3
                if stage == 0:
                    for src, srcB, dst, dstB in (((k_t, aB["k_t"], bm_k2[sub],
                                                   aB[f"bm_k{sub}"]),
                                                 (v_b, aB["v_b"], bm_v2[sub],
                                                  aB[f"bm_v{sub}"]))
                                                 if "b" in KCHP else ()):
                        for dc in range(8):
                            bank, pbk = pB[dc % 4], pb[dc % 4]
                            for lvl in range(NLVL):
                                lh = src[:, (g % 3) * 1024 + dc * 128:
                                         (g % 3) * 1024 + (dc + 1) * 128]
                                sch.pe(lambda e, lh=lh, lvl=lvl, pbk=pbk: e.matmul(
                                    pbk[:, lvl * 128:(lvl + 1) * 128], lh,
                                    mt[:, lvl * 128:(lvl + 1) * 128],
                                    start=True, stop=True),
                                    reads=[srcB, gB["mt"]], writes=[bank])
                            sch.act(lambda e, dst=dst, dc=dc, pbk=pbk: e.copy(
                                dst[:, dc * 512:(dc + 1) * 512], pbk[:, 0:512]),
                                reads=[bank], writes=[(dstB, dc)])

                elif stage == 1:
                    for w in range(2 if "l" in KCHP else 0):
                        for j in range(8):
                            h = w * 8 + j
                            bank, pbk = pB[j // 4], pb[j // 4]
                            sch.pe(lambda e, pbk=pbk, j=j, h=h, slot=slot, sub=sub: e.matmul(
                                pbk[:, (j % 4) * 128:(j % 4 + 1) * 128],
                                klr[(h % 2) * 64:(h % 2) * 64 + 64,
                                    slot * 1024 + (h // 2) * 128:slot * 1024 + (h // 2) * 128 + 128],
                                qsl(h, g), start=True, stop=True),
                                reads=[aB["klr"], aB["q_f"]], writes=[bank])
                            pbx, pbxB = (pb[2], pB[2]) if j < 4 else (pb[4], pB[4])
                            sch.pe(lambda e, j=j, h=h, pslot=pslot, sub=sub,
                                   pbx=pbx: e.matmul(
                                pbx[64:128, (j % 4) * 128:(j % 4) * 128 + 128],
                                klr[(h % 2) * 64:(h % 2) * 64 + 64,
                                    pslot * 1024 + (h // 2) * 128 + 64:
                                    pslot * 1024 + (h // 2) * 128 + 128],
                                qsl(h, g), start=True, stop=True),
                                reads=[aB["klr"], aB["q_f"]], writes=[pbxB])
                        for half in range(2):
                            bank, pbk = pB[half], pb[half]
                            sch.dve(lambda e, pbk=pbk: e.tensor_tensor(
                                out=scl[:, 0:512].rearrange("p (j t) -> p j t", t=128),
                                in0=pbk[:, 0:512].rearrange("p (j t) -> p j t", t=128),
                                in1=maskd[:].unsqueeze(1).broadcast_to([128, 4, 128]),
                                op=ALU.add),
                                reads=[bank, gB["maskd"]], writes=[aB["scl"]])
                            sch.act(lambda e, half=half: e.activation(
                                out=ed[:, half * 512:(half + 1) * 512], in_=scl[:, 0:512],
                                func=AF.Exp, scale=0.125),
                                reads=[aB["scl"]], writes=[(aB["ed"], half)])
                        mp_use = maskp if g > 0 else maskp0
                        for pk in range(2):
                            pbx, pbxB = (pb[2], pB[2]) if pk == 0 else (pb[4], pB[4])
                            sch.dve(lambda e, pbx=pbx, mp_use=mp_use: e.tensor_tensor(
                                out=scl[64:128, 0:512].rearrange("p (j t) -> p j t", t=128),
                                in0=pbx[64:128, 0:512].rearrange("p (j t) -> p j t", t=128),
                                in1=mp_use[64:128, :].unsqueeze(1).broadcast_to([64, 4, 128]),
                                op=ALU.add),
                                reads=[pbxB, gB["maskp"]], writes=[aB["scl"]])
                            sch.act(lambda e, pk=pk: e.activation(
                                out=ep[64:128, pk * 512:(pk + 1) * 512],
                                in_=scl[64:128, 0:512], func=AF.Exp, scale=0.125),
                                reads=[aB["scl"]], writes=[aB["ep"]])
                        for j in range(8):
                            h = w * 8 + j
                            obk, obkB = (pb[3], pB[3]) if j < 4 else (pb[5], pB[5])
                            sch.pe(lambda e, j=j, h=h, slot=slot, obk=obk: e.matmul(
                                obk[:, (j % 4) * 65:(j % 4) * 65 + 65],
                                ed[:, j * 128:(j + 1) * 128],
                                vlr[:, slot * 1040 + h * 65:slot * 1040 + h * 65 + 65],
                                start=True, stop=False),
                                reads=[(aB["ed"], j // 4), aB["vlr"]], writes=[obkB])
                            sch.pe(lambda e, j=j, h=h, pslot=pslot, obk=obk: e.matmul(
                                obk[:, (j % 4) * 65:(j % 4) * 65 + 65],
                                ep[64:128, j * 128:(j + 1) * 128],
                                vlr[64:128, pslot * 1040 + h * 65:pslot * 1040 + h * 65 + 65],
                                start=False, stop=True),
                                reads=[aB["ep"], aB["vlr"]], writes=[obkB])
                        for bk2 in range(2):
                            obk, obkB = (pb[3], pB[3]) if bk2 == 0 else (pb[5], pB[5])
                            sch.dve(lambda e, w=w, bk2=bk2, obk=obk: e.tensor_scalar(
                                out=scrF[:, 16 + w * 8 + bk2 * 4:20 + w * 8 + bk2 * 4]
                                .unsqueeze(2),
                                in0=obk[:, 0:260].rearrange("p (a b) -> p a b", b=65)
                                [:, :, 64:65],
                                scalar1=EPS_LA, scalar2=None, op0=ALU.max),
                                reads=[obkB], writes=[(aB["scrF"], "l")])
                        sch.dve(lambda e, w=w: e.reciprocal(
                            out=scrF[:, 16 + w * 8:24 + w * 8],
                            in_=scrF[:, 16 + w * 8:24 + w * 8]),
                            reads=[(aB["scrF"], "l")], writes=[(aB["scrF"], "l")])
                        for bk2 in range(2):
                            obk, obkB = (pb[3], pB[3]) if bk2 == 0 else (pb[5], pB[5])
                            sch.dve(lambda e, w=w, sub=sub, bk2=bk2, obk=obk: e.tensor_tensor(
                                out=loc[:, sub * 1024 + w * 512 + bk2 * 256:
                                        sub * 1024 + w * 512 + bk2 * 256 + 256]
                                .rearrange("p (h z) -> p h z", z=64),
                                in0=obk[:, 0:260].rearrange("p (a b) -> p a b", b=65)
                                [:, :, 0:64],
                                in1=scrF[:, 16 + w * 8 + bk2 * 4:20 + w * 8 + bk2 * 4]
                                .unsqueeze(2).broadcast_to([128, 4, 64]),
                                op=ALU.mult),
                                reads=[obkB, (aB["scrF"], "l")], writes=[(aB["loc"], sub)])
                    sch.dma("ACT", lambda e, g=g, sub=sub: e.dma_start(
                        out=local_scr[g * 128:(g + 1) * 128, :],
                        in_=loc[:, sub * 1024:(sub + 1) * 1024]),
                        reads=[(aB["loc"], sub)], writes=[(dB["local_scr"], g)])

                elif stage == 2:
                    if "p" in KCHP:
                        phi_inplace(q_f, aB["q_f"], (g % 3) * 1024, 1024, sub)
                        phi_inplace(k_t, aB["k_t"], (g % 3) * 1024, 1024, sub)
                    for grp in range(2 if "p" in KCHP else 0):
                        bank, pbk = pB[grp], pb[grp]
                        for j in range(4):
                            dc = grp * 4 + j
                            sch.pe(lambda e, dc=dc, j=j, pbk=pbk, g=g: e.matmul(
                                pbk[:, j * 128:(j + 1) * 128],
                                k_t[:, (g % 3) * 1024 + dc * 128:
                                    (g % 3) * 1024 + (dc + 1) * 128],
                                ident[:], start=True, stop=True),
                                reads=[aB["k_t"], gB["ident"]], writes=[bank])
                        sch.act(lambda e, grp=grp, pbk=pbk, g=g: e.copy(
                            kp_f[:, (g % 2) * 1024 + grp * 512:
                                 (g % 2) * 1024 + (grp + 1) * 512], pbk[:, 0:512]),
                            reads=[bank], writes=[(aB["kp_f"], (g % 2, grp))])


            def chunk_calls(s, sub, cbs=()):
                cbs = list(cbs)

                def pop_cb():
                    if cbs:
                        cb = cbs.pop(0)
                        if cb is not None:
                            cb()

                KCHP = os.environ.get("KCHP", "blpmhs")
                g = 2 * s + sub
                slot, pslot = g % 3, (g - 1) % 3
                def produce_lvl(lvl):
                    par = (4 * g + lvl) % 3
                    lB = lvB[par]
                    for which, wmat, wmatB, tok in (
                            ("kF", bdk, gB["bdk"], False),
                            ("kT", bdk, gB["bdk"], True),
                            ("vT", bdv, gB["bdv"], True)):
                        bm, bmB = ((bm_k2[sub], aB[f"bm_k{sub}"]) if which[0] == "k"
                                   else (bm_v2[sub], aB[f"bm_v{sub}"]))
                        for grp in range(2):
                            bank, pbk = pB[1 + 2 * grp], pb[1 + 2 * grp]
                            for j in range(4):
                                dc = grp * 4 + j
                                if not tok:
                                    sch.pe(lambda e, lvl=lvl, dc=dc, j=j, pbk=pbk,
                                           wmat=wmat, bm=bm: e.matmul(
                                        pbk[:, j * 128:(j + 1) * 128],
                                        wmat[:, lvl * 128:(lvl + 1) * 128],
                                        bm[:, dc * 512 + lvl * 128:dc * 512 + (lvl + 1) * 128],
                                        start=True, stop=True),
                                        reads=[wmatB, (bmB, dc)], writes=[bank])
                                else:
                                    sch.pe(lambda e, lvl=lvl, dc=dc, j=j, pbk=pbk,
                                           wmat=wmat, bm=bm: e.matmul(
                                        pbk[:, j * 128:(j + 1) * 128],
                                        bm[:, dc * 512 + lvl * 128:dc * 512 + (lvl + 1) * 128],
                                        wmat[:, lvl * 128:(lvl + 1) * 128],
                                        start=True, stop=True),
                                        reads=[wmatB, (bmB, dc)], writes=[bank])
                            if which == "vT":
                                sch.act(lambda e, grp=grp, pbk=pbk, par=par: e.copy(
                                    vlv[par][:, grp * 520:(grp + 1) * 520]
                                    .rearrange("p (h z) -> p h z", z=65)[:, :, 0:64],
                                    pbk[:, 0:512].rearrange("p (h z) -> p h z", z=64)),
                                    reads=[bank], writes=[(lB["vlv"], grp)])
                            else:
                                off = (0 if which == "kF" else 1024) + grp * 512
                                sch.act(lambda e, off=off, pbk=pbk, par=par: e.copy(
                                    klv2[par][:, off:off + 512], pbk[:, 0:512]),
                                    reads=[bank], writes=[(lB["klv"], (which, grp))])
                    phi_inplace(klv2[par], lB["klv"], 0, 2048, sub)



                if "m" in KCHP:
                    if "h" in KCHP:
                        produce_lvl(0)
                        produce_lvl(1)
                    la_call(g, 0, kp_f, aB["kp_f"], (g % 2) * 1024,
                            k_t, aB["k_t"], (g % 3) * 1024,
                            v_t, aB["v_t"], (g % 3) * 1040,
                            prod=(lambda: produce_lvl(2)) if "h" in KCHP else None)
                    pop_cb()
                for lvl in range(NLVL if "h" in KCHP else 0):
                    par = (4 * g + lvl) % 3
                    lB = lvB[par]
                    la_call(g, lvl + 1, klv2[par], lB["klv"], 0,
                            klv2[par], lB["klv"], 1024, vlv[par], lB["vlv"], 0,
                            prod=(lambda lvl=lvl: produce_lvl(lvl + 3))
                            if lvl < NLVL - 3 else None)
                    pop_cb()
                while cbs:
                    pop_cb()
                sch.act(lambda e, sub=sub: e.copy(
                    glob_bf[:],
                    glob[:, sub * 1024:(sub + 1) * 1024]),
                    reads=[(aB["glob"], (sub, 0)), (aB["glob"], (sub, 1)),
                           (aB["glob"], (sub, 2)), (aB["glob"], (sub, 3))],
                    writes=[aB["glob_bf"]])
                sch.dma("ACT", lambda e, g=g: e.dma_start(
                    out=glob_scr[g * 128:(g + 1) * 128, :],
                    in_=glob_bf[:]),
                    reads=[aB["glob_bf"]], writes=[(dB["glob_scr"], g)])
                if "s" not in KCHP:
                    return
                sch.act(lambda e: e.copy(sh_kv[:], s_all[:]),
                        reads=[aB["s_all"]], writes=[aB["sh_kv"]])

            def prep_ln(s):
                T0 = s * 256
                sch.dma("SP", lambda e, T0=T0: e.dma_start(
                    out=x_t[:].rearrange("p (s f) -> p s f", f=1024),
                    in_=x_d[T0:T0 + 256, :].rearrange("(s p) f -> p s f", p=128)),
                    writes=[aB["x_t"]])
                for sub in range(2):
                    sch.act(lambda e, sub=sub: e.activation(
                        out=scrP2[0][:, 0:1024], in_=x_t[:, sub * 1024:(sub + 1) * 1024],
                        func=AF.Square, accum_out=stats[:, sub:sub + 1]),
                        reads=[aB["x_t"]], writes=[aB["scrP0"], aB["stats"]])
                    sch.act(lambda e, sub=sub: e.activation(
                        out=scrP2[0][:, 0:1024], in_=x_t[:, sub * 1024:(sub + 1) * 1024],
                        func=AF.Copy, accum_out=stats[:, 2 + sub:3 + sub]),
                        reads=[aB["x_t"]], writes=[aB["scrP0"], aB["stats"]])
                sch.dve(lambda e: e.tensor_scalar(
                    out=stats[:, 4:6], in0=stats[:, 2:4], scalar1=-1.0 / DM,
                    scalar2=None, op0=ALU.mult),
                    reads=[aB["stats"]], writes=[aB["stats"]])
                sch.dve(lambda e: e.tensor_tensor(
                    out=stats[:, 6:8], in0=stats[:, 4:6], in1=stats[:, 4:6],
                    op=ALU.mult),
                    reads=[aB["stats"]], writes=[aB["stats"]])
                sch.dve(lambda e: e.tensor_scalar(
                    out=stats[:, 0:2], in0=stats[:, 0:2], scalar1=1.0 / DM,
                    scalar2=None, op0=ALU.mult),
                    reads=[aB["stats"]], writes=[aB["stats"]])
                sch.dve(lambda e: e.tensor_tensor(
                    out=stats[:, 6:8], in0=stats[:, 0:2], in1=stats[:, 6:8],
                    op=ALU.subtract),
                    reads=[aB["stats"]], writes=[aB["stats"]])
                sch.dve(lambda e: e.tensor_scalar(
                    out=stats[:, 6:8], in0=stats[:, 6:8], scalar1=EPS_LN,
                    scalar2=None, op0=ALU.add),
                    reads=[aB["stats"]], writes=[aB["stats"]])
                sch.act(lambda e: e.activation(out=stats[:, 8:10], in_=stats[:, 6:8],
                                               func=AF.Sqrt),
                        reads=[aB["stats"]], writes=[aB["stats"]])
                sch.dve(lambda e: e.reciprocal(out=stats[:, 8:10], in_=stats[:, 8:10]),
                        reads=[aB["stats"]], writes=[aB["stats"]])
                for sub in range(2):
                    sch.dve(lambda e, sub=sub: e.tensor_scalar(
                        out=xn_t[:, sub * 1024:(sub + 1) * 1024],
                        in0=x_t[:, sub * 1024:(sub + 1) * 1024],
                        scalar1=stats[:, 4 + sub:5 + sub],
                        scalar2=stats[:, 8 + sub:9 + sub],
                        op0=ALU.add, op1=ALU.mult),
                        reads=[aB["x_t"], aB["stats"]], writes=[(aB["xn_t"], sub)])

            def prep_xp(s):
                T0 = s * 256
                for grp in range(4):
                    bank, pbk = pB[grp], pb[grp]
                    idx0 = grp * 4
                    sub0, c0 = idx0 // 8, idx0 % 8
                    for j in range(4):
                        idx = idx0 + j
                        sb_, c_ = idx // 8, idx % 8
                        sch.pe(lambda e, pbk=pbk, j=j, sb_=sb_, c_=c_: e.matmul(
                            pbk[:, j * 128:(j + 1) * 128],
                            xn_t[:, sb_ * 1024 + c_ * 128:sb_ * 1024 + (c_ + 1) * 128],
                            ident[:], start=True, stop=True),
                            reads=[(aB["xn_t"], sb_), gB["ident"]], writes=[bank])
                    sch.act(lambda e, pbk=pbk, sub0=sub0, c0=c0: e.copy(
                        xn_f[:].rearrange("p (c t) -> p c t", t=256)
                        [:, c0:c0 + 4, sub0 * 128:(sub0 + 1) * 128],
                        pbk[:].rearrange("p (j t) -> p j t", t=128)),
                        reads=[bank], writes=[(aB["xn_f"], (sub0, c0))])
                sch.dma("ACT", lambda e, T0=T0: e.dma_start(
                    out=xn_scr[:, :, T0:T0 + 256].transpose([1, 0, 2]),
                    in_=xn_f[:].rearrange("p (c t) -> p c t", t=256)),
                    reads=[aB["xn_f"]], writes=[(dB["xn_scr"], s)])

            def proj_feat(s, wname, featdst, wi):
                worder = ("wq", "wkl", "wk", "wv", "wvl")
                if wi == 0:
                    load_proj(s, 0, "wq")
                load_proj(s, wi + 1, worder[wi + 1])
                wsl = wr_slot_of[wname]
                if True:
                    for half in range(2):
                        for ocg in range(4):
                            oc = half * 4 + ocg
                            for ic in range(8):
                                sch.pe(lambda e, wsl=wsl, oc=oc, ocg=ocg,
                                       ic=ic: e.matmul(
                                    pb[ocg][:, 0:256],
                                    WR[wsl][:, ic * 1024 + oc * 128:
                                            ic * 1024 + (oc + 1) * 128],
                                    xn_f[:, ic * 256:(ic + 1) * 256],
                                    start=(ic == 0), stop=(ic == 7)),
                                    reads=[WRB[wsl], aB["xn_f"]], writes=[pB[ocg]])
                        for ocg in range(4):
                            oc = half * 4 + ocg
                            if featdst == "q":
                                for sb2 in range(2):
                                    slot2 = (2 * s + sb2) % 3
                                    sch.act(lambda e, oc=oc, ocg=ocg, sb2=sb2,
                                            slot2=slot2: e.copy(
                                        q_f[:, slot2 * 1024 + oc * 128:
                                            slot2 * 1024 + (oc + 1) * 128],
                                        pb[ocg][:, sb2 * 128:(sb2 + 1) * 128]),
                                        reads=[pB[ocg]], writes=[aB["q_f"]])
                            else:
                                for sub in range(2):
                                    slot = (2 * s + sub) % 3
                                    sch.act(lambda e, oc=oc, ocg=ocg, sub=sub,
                                            slot=slot: e.copy(
                                        klr[:, slot * 1024 + oc * 128:
                                            slot * 1024 + (oc + 1) * 128],
                                        pb[ocg][:, sub * 128:(sub + 1) * 128]),
                                        reads=[pB[ocg]], writes=[aB["klr"]])
            def proj_tok(s, wname, dest, destB, wi):
                worder = ("wq", "wkl", "wk", "wv", "wvl")
                if True:
                    if 3 + wi < 5:
                        load_proj(s, 3 + wi, worder[3 + wi])
                    wsl = wr_slot_of[wname]
                    stride65 = wname in ("wv", "wvl")
                    for sub in range(2):
                        slot = (2 * s + sub) % 3
                        base = slot * 1040 if stride65 else slot * 1024
                        for half in range(2):
                            bank, pbk = pB[4 + half], pb[4 + half]
                            for ic in range(8):
                                sch.pe(lambda e, wsl=wsl, sub=sub, half=half,
                                       ic=ic, pbk=pbk: e.matmul(
                                    pbk[:, 0:512],
                                    xn_f[:, ic * 256 + sub * 128:
                                         ic * 256 + sub * 128 + 128],
                                    WR[wsl][:, ic * 1024 + half * 512:
                                            ic * 1024 + half * 512 + 512],
                                    start=(ic == 0), stop=(ic == 7)),
                                    reads=[WRB[wsl], aB["xn_f"]], writes=[bank])
                            if stride65:
                                sch.act(lambda e, dest=dest, base=base, half=half,
                                        pbk=pbk: e.copy(
                                    dest[:, base + half * 520:base + half * 520 + 520]
                                    .rearrange("p (h z) -> p h z", z=65)[:, :, 0:64],
                                    pbk[:, 0:512].rearrange("p (h z) -> p h z", z=64)),
                                    reads=[bank], writes=[destB])
                                if wname == "wv":
                                    sch.act(lambda e, slot=slot, half=half, pbk=pbk: e.copy(
                                        v_b[:, slot * 1024 + half * 512:
                                            slot * 1024 + half * 512 + 512],
                                        pbk[:, 0:512]),
                                        reads=[bank], writes=[aB["v_b"]])
                            else:
                                sch.act(lambda e, dest=dest, base=base, half=half,
                                        pbk=pbk: e.copy(
                                    dest[:, base + half * 512:base + half * 512 + 512],
                                    pbk[:, 0:512]),
                                    reads=[bank], writes=[destB])
            def proj_all(s):
                proj_feat(s, "wq", "q", 0)
                proj_feat(s, "wkl", "kl", 1)
                proj_tok(s, "wk", k_t, aB["k_t"], 0)
                proj_tok(s, "wv", v_t, aB["v_t"], 1)
                proj_tok(s, "wvl", vlr, aB["vlr"], 2)

            def phase_a_body(s):
                if s + 1 < nA:
                    prep_ln(s + 1)
                chunk_calls(s, 0, cbs=[
                    lambda: chunk_prep(s, 1, 0),
                    lambda: chunk_prep(s, 1, 1),
                    lambda: chunk_prep(s, 1, 2),
                ])
                if s + 1 < nA:
                    prep_xp(s + 1)
                    chunk_calls(s, 1, cbs=[
                        lambda: (proj_feat(s + 1, "wq", "q", 0),
                                 proj_feat(s + 1, "wkl", "kl", 1)),
                        lambda: (proj_tok(s + 1, "wk", k_t, aB["k_t"], 0),
                                 proj_tok(s + 1, "wv", v_t, aB["v_t"], 1)),
                        lambda: proj_tok(s + 1, "wvl", vlr, aB["vlr"], 2),
                        lambda: chunk_prep(s + 1, 0, 0),
                        lambda: chunk_prep(s + 1, 0, 1),
                        lambda: chunk_prep(s + 1, 0, 2),
                    ])
                else:
                    chunk_calls(s, 1, cbs=[])

            conv_per = max(1, (n_conv_left + NSUPA - 1) // NSUPA)
            nA = NSUPA if "A" in KPH else 0
            nA = min(nA, int(os.environ.get("KSUPA", "999")))
            if nA > 0:
                prep_ln(0)
                prep_xp(0)
                proj_all(0)
                for st in range(3):
                    chunk_prep(0, 0, st)
            for s in range(nA):
                phase_a_body(s)
                emit_conv(conv_per)
                sch.flush(engmap)
            emit_conv(10000)
            sch.flush(engmap)
            sch.barrier()

        with contextlib.ExitStack() as pbs:
            EB = pbs.enter_context
            wg_sb = EB(nc.sbuf_tensor("wg_sb", [128, 16 * DM], BF16))
            wo_sb = EB(nc.sbuf_tensor("wo_sb", [128, 8 * DM], BF16))
            xnf_b = EB(nc.sbuf_tensor("xnf_b", [128, 8 * 512], BF16))
            gl_b = EB(nc.sbuf_tensor("gl_b", [128, 4 * 1024], BF16))
            lo_b = EB(nc.sbuf_tensor("lo_b", [128, 4 * 1024], BF16))
            df_f = EB(nc.sbuf_tensor("df_f", [128, 8 * 512], BF16))
            gt_f = EB(nc.sbuf_tensor("gt_f", [128, 8 * 512], BF16))
            mx_f = EB(nc.sbuf_tensor("mx_f", [128, 8 * 512], BF16))
            xb_t = EB(nc.sbuf_tensor("xb_t", [128, 4 * 1024], F32))
            al_r = EB(nc.sbuf_tensor("al_r", [1, 512], F32))
            al_c = EB(nc.sbuf_tensor("al_c", [128, 4], F32))
            sco = EB(nc.sbuf_tensor("sco", [128, 1024], F32))
            scm = EB(nc.sbuf_tensor("scm", [128, 1024], BF16))
            outs = EB(nc.sbuf_tensor("outs", [128, 1024], F32))
            bo_b = EB(nc.sbuf_tensor("bo_b", [128, DM], F32))

            bB = {n: Buf(n) for n in ["wg_sb", "wo_sb", "xnf_b", "gl_b", "lo_b",
                                      "df_f", "gt_f", "mx_f", "xb_t", "al_r",
                                      "al_c", "sco", "scm", "outs"]}

            ld(bo_b[:], dt["bo_bc"], writes=[gB["bo_b"]])
            sch.dma("SP", lambda e: e.dma_start(
                out=wg_sb[:].rearrange("p (i f) -> p i f", f=DM),
                in_=wg_scr[:, :, :].transpose([1, 0, 2])),
                reads=[dB["wg_scr"]], writes=[bB["wg_sb"]])
            sch.dma("SP", lambda e: e.dma_start(
                out=wo_sb[:].rearrange("p (i f) -> p i f", f=DM),
                in_=wo_scr[:, :, :].transpose([1, 0, 2])),
                reads=[dB["wo_scr"]], writes=[bB["wo_sb"]])

            def phase_b_supertile(sb):
                T0 = sb * 512
                sch.dma("SP", lambda e, T0=T0: e.dma_start(
                    out=gl_b[:].rearrange("p (s f) -> p s f", f=1024),
                    in_=glob_scr[T0:T0 + 512, :]
                    .rearrange("(s p) f -> p s f", p=128)),
                    reads=[dB["glob_scr"]], writes=[bB["gl_b"]])
                sch.dma("SP", lambda e, T0=T0: e.dma_start(
                    out=lo_b[:].rearrange("p (s f) -> p s f", f=1024),
                    in_=local_scr[T0:T0 + 512, :]
                    .rearrange("(s p) f -> p s f", p=128)),
                    reads=[dB["local_scr"]], writes=[bB["lo_b"]])
                sch.dma("SP", lambda e, T0=T0: e.dma_start(
                    out=xnf_b[:].rearrange("p (c t) -> p c t", t=512),
                    in_=xn_scr[:, :, T0:T0 + 512].transpose([1, 0, 2])),
                    reads=[dB["xn_scr"]], writes=[bB["xnf_b"]])
                sch.dma("SP", lambda e, T0=T0: e.dma_start(
                    out=xb_t[:].rearrange("p (s f) -> p s f", f=1024),
                    in_=x_d[T0:T0 + 512, :]
                    .rearrange("(s p) f -> p s f", p=128)),
                    writes=[bB["xb_t"]])
                sch.dve(lambda e: e.tensor_tensor(out=lo_b[:], in0=lo_b[:],
                                                  in1=gl_b[:], op=ALU.subtract),
                        reads=[bB["gl_b"], bB["lo_b"]], writes=[bB["lo_b"]])
                for c in range(8):
                    bank, pbk = pB[c % 4], pb[c % 4]
                    for sq in range(4):
                        sch.pe(lambda e, c=c, sq=sq, pbk=pbk: e.matmul(
                            pbk[:, sq * 128:(sq + 1) * 128],
                            lo_b[:, sq * 1024 + c * 128:sq * 1024 + (c + 1) * 128],
                            ident[:], start=True, stop=True),
                            reads=[bB["lo_b"], gB["ident"]], writes=[bank])
                    sch.act(lambda e, c=c, pbk=pbk: e.copy(
                        df_f[:, c * 512:(c + 1) * 512], pbk[:, 0:512]),
                        reads=[bank], writes=[(bB["df_f"], c)])
                for oc in range(8):
                    bank, pbk = pB[4 + oc % 2], pb[4 + oc % 2]
                    for ic in range(8):
                        sch.pe(lambda e, oc=oc, ic=ic, pbk=pbk: e.matmul(
                            pbk[:, 0:512],
                            wg_sb[:, ic * 1024 + oc * 128:ic * 1024 + (oc + 1) * 128],
                            xnf_b[:, ic * 512:(ic + 1) * 512],
                            start=(ic == 0), stop=False),
                            reads=[bB["wg_sb"], bB["xnf_b"]], writes=[bank])
                    for ic in range(8):
                        sch.pe(lambda e, oc=oc, ic=ic, pbk=pbk: e.matmul(
                            pbk[:, 0:512],
                            wg_sb[:, (8 + ic) * 1024 + oc * 128:
                                  (8 + ic) * 1024 + (oc + 1) * 128],
                            df_f[:, ic * 512:(ic + 1) * 512],
                            start=False, stop=(ic == 7)),
                            reads=[bB["wg_sb"], (bB["df_f"], ic)], writes=[bank])
                    sch.act(lambda e, oc=oc, pbk=pbk: e.activation(
                        out=gt_f[:, oc * 512:(oc + 1) * 512], in_=pbk[:, 0:512],
                        func=AF.Silu, bias=bg_sb[:, oc:oc + 1], scale=1.0),
                        reads=[bank, gB["bg_sb"]], writes=[(bB["gt_f"], oc)])
                for oc in range(8):
                    sch.pe(lambda e, oc=oc: e.matmul(
                        pb[6][0:1, 0:512], wgoT_bf[:, oc:oc + 1],
                        gt_f[:, oc * 512:(oc + 1) * 512],
                        start=(oc == 0), stop=(oc == 7)),
                        reads=[(bB["gt_f"], oc), gB["wgoT"]], writes=[pB[6]])
                sch.act(lambda e: e.activation(out=al_r[:], in_=pb[6][0:1, 0:512],
                                               func=AF.Sigmoid, bias=bgo_sb[:],
                                               scale=1.0),
                        reads=[pB[6], gB["bgo_sb"]], writes=[bB["al_r"]])
                for sq in range(4):
                    sch.pe(lambda e, sq=sq: e.matmul(
                        pb[7][:, sq:sq + 1], al_r[0:1, sq * 128:(sq + 1) * 128],
                        onef[:], start=True, stop=True),
                        reads=[bB["al_r"], gB["onef"]], writes=[pB[7]])
                sch.dve(lambda e: e.tensor_copy(al_c[:], pb[7][:, 0:4]),
                        reads=[pB[7]], writes=[bB["al_c"]])
                for sq in range(4):
                    sch.dve(lambda e, sq=sq: e.tensor_scalar(
                        out=scm[:], in0=lo_b[:, sq * 1024:(sq + 1) * 1024],
                        scalar1=al_c[:, sq:sq + 1], scalar2=None, op0=ALU.mult),
                        reads=[bB["lo_b"], bB["al_c"]], writes=[bB["scm"]])
                    sch.dve(lambda e, sq=sq: e.tensor_tensor(
                        out=gl_b[:, sq * 1024:(sq + 1) * 1024],
                        in0=gl_b[:, sq * 1024:(sq + 1) * 1024], in1=scm[:],
                        op=ALU.add),
                        reads=[bB["scm"], bB["gl_b"]], writes=[bB["gl_b"]])
                for c in range(8):
                    bank, pbk = pB[c % 4], pb[c % 4]
                    for sq in range(4):
                        sch.pe(lambda e, c=c, sq=sq, pbk=pbk: e.matmul(
                            pbk[:, sq * 128:(sq + 1) * 128],
                            gl_b[:, sq * 1024 + c * 128:sq * 1024 + (c + 1) * 128],
                            ident[:], start=True, stop=True),
                            reads=[bB["gl_b"], gB["ident"]], writes=[bank])
                    sch.act(lambda e, c=c, pbk=pbk: e.copy(
                        mx_f[:, c * 512:(c + 1) * 512], pbk[:, 0:512]),
                        reads=[bank], writes=[(bB["mx_f"], c)])
                for sq in range(4):
                    for half in range(2):
                        bank, pbk = pB[4 + half], pb[4 + half]
                        for ic in range(8):
                            sch.pe(lambda e, sq=sq, half=half, ic=ic, pbk=pbk: e.matmul(
                                pbk[:, 0:512],
                                mx_f[:, ic * 512 + sq * 128:ic * 512 + sq * 128 + 128],
                                wo_sb[:, ic * 1024 + half * 512:
                                      ic * 1024 + half * 512 + 512],
                                start=(ic == 0), stop=(ic == 7)),
                                reads=[(bB["mx_f"], ic), bB["wo_sb"]], writes=[bank])
                        sch.dve(lambda e, sq=sq, half=half, pbk=pbk: e.tensor_tensor(
                            out=sco[:, half * 512:(half + 1) * 512], in0=pbk[:, 0:512],
                            in1=xb_t[:, sq * 1024 + half * 512:
                                     sq * 1024 + half * 512 + 512],
                            op=ALU.add),
                            reads=[bank, bB["xb_t"]], writes=[(bB["sco"], half)])
                        sch.dve(lambda e, half=half: e.tensor_tensor(
                            out=outs[:, half * 512:(half + 1) * 512],
                            in0=sco[:, half * 512:(half + 1) * 512],
                            in1=bo_b[:, half * 512:(half + 1) * 512], op=ALU.add),
                            reads=[(bB["sco"], half), gB["bo_b"]],
                            writes=[(bB["outs"], half)])
                    sch.dma("ACT", lambda e, T0=T0, sq=sq: e.dma_start(
                        out=out_d[T0 + sq * 128:T0 + (sq + 1) * 128, :], in_=outs[:]),
                        reads=[(bB["outs"], 0), (bB["outs"], 1)],
                        writes=[(dB["out"], (sb, sq))])

            for sb in range(NSUPB if "B" in KPH else 0):
                phase_b_supertile(sb)
                sch.flush(engmap)
            sch.flush(engmap)
            sch.barrier()

        with contextlib.ExitStack() as pcs:
            EC = pcs.enter_context
            h_a = EC(nc.sbuf_tensor("h_a", [128, 32 * CT], BF16))
            h_b = EC(nc.sbuf_tensor("h_b", [128, 32 * CT], BF16))
            xnf_c = h_b
            wA = [EC(nc.sbuf_tensor(f"wAa{i}", [128, 8 * 512], BF16)) for i in range(2)]
            wb2 = [EC(nc.sbuf_tensor(f"wBb{i}", [128, 8 * 512], BF16)) for i in range(2)]
            wH = [EC(nc.sbuf_tensor(f"wHh{i}", [128, 32 * 256], BF16)) for i in range(2)]
            w3sb = wH
            scs = EC(nc.sbuf_tensor("scs", [128, 512], BF16))
            sct = EC(nc.sbuf_tensor("sct", [128, 512], BF16))
            outc = EC(nc.sbuf_tensor("outc", [128, 256], F32))
            w3sc = EC(nc.sbuf_tensor("w3sc", [128, 256], F32))

            cB = {n: Buf(n) for n in ["h_a", "h_b", "scs", "sct", "outc"]}
            cB["xnf_c"] = cB["h_b"]
            wAB = [Buf(f"wAa{i}") for i in range(2)]
            wBB = [Buf(f"wBb{i}") for i in range(2)]
            wHB = [Buf(f"wHh{i}") for i in range(2)]
            w3B = wHB

            def phase_c_tile(tc):
                T0 = tc * CT
                NTH = CT // 512
                sch.dma("SP", lambda e, T0=T0: e.dma_start(
                    out=xnf_c[:, 0:8 * CT].rearrange("p (c t) -> p c t", t=CT),
                    in_=xn_scr[:, :, T0:T0 + CT].transpose([1, 0, 2])),
                    reads=[dB["xn_scr"]], writes=[cB["xnf_c"]])
                for og in range(8):
                    par = og % 2
                    sch.dma("SP", lambda e, og=og, par=par: e.dma_start(
                        out=wA[par][:].rearrange("p (i f) -> p i f", f=512),
                        in_=w1_scr[:, :, og * 512:(og + 1) * 512].transpose([1, 0, 2])),
                        reads=[dB["w1_scr"]], writes=[wAB[par]])
                    sch.dma("SP", lambda e, og=og, par=par: e.dma_start(
                        out=wb2[par][:].rearrange("p (i f) -> p i f", f=512),
                        in_=w2_scr[:, :, og * 512:(og + 1) * 512].transpose([1, 0, 2])),
                        reads=[dB["w2_scr"]], writes=[wBB[par]])
                    for oci in range(4):
                        oc = og * 4 + oci
                        for th in range(NTH):
                            b1, p1 = pB[2 * (oci % 2)], pb[2 * (oci % 2)]
                            b2, p2 = pB[2 * (oci % 2) + 1], pb[2 * (oci % 2) + 1]
                            for ic in range(8):
                                sch.pe(lambda e, par=par, oci=oci, ic=ic, th=th,
                                       p1=p1: e.matmul(
                                    p1[:, 0:512],
                                    wA[par][:, ic * 512 + oci * 128:
                                            ic * 512 + (oci + 1) * 128],
                                    xnf_c[:, ic * CT + th * 512:ic * CT + th * 512 + 512],
                                    start=(ic == 0), stop=(ic == 7)),
                                    reads=[wAB[par], cB["xnf_c"]], writes=[b1])
                            for ic in range(8):
                                sch.pe(lambda e, par=par, oci=oci, ic=ic, th=th,
                                       p2=p2: e.matmul(
                                    p2[:, 0:512],
                                    wb2[par][:, ic * 512 + oci * 128:
                                             ic * 512 + (oci + 1) * 128],
                                    xnf_c[:, ic * CT + th * 512:ic * CT + th * 512 + 512],
                                    start=(ic == 0), stop=(ic == 7)),
                                    reads=[wBB[par], cB["xnf_c"]], writes=[b2])
                            sch.act(lambda e, oc=oc, p1=p1: e.activation(
                                out=scs[:], in_=p1[:, 0:512], func=AF.Silu,
                                scale=s_w1s[:, oc:oc + 1]),
                                reads=[b1, gB["w1s"]], writes=[cB["scs"]])
                            sch.dve(lambda e, oc=oc, p2=p2: e.tensor_scalar(
                                out=sct[:], in0=p2[:, 0:512],
                                scalar1=s_w2s[:, oc:oc + 1], scalar2=None, op0=ALU.mult),
                                reads=[b2, gB["w2s"]], writes=[cB["sct"]])
                            sch.dve(lambda e, oc=oc, th=th: e.tensor_tensor(
                                out=h_a[:, oc * CT + th * 512:oc * CT + th * 512 + 512],
                                in0=scs[:], in1=sct[:], op=ALU.mult),
                                reads=[cB["scs"], cB["sct"]],
                                writes=[(cB["h_a"], (oc, th))])

                def hstage(wscr, wscrB, src, srcB, dst, dstB, s_col, sB1, s_e, sB2):
                    for og in range(16):
                        par = og % 2
                        sch.dma("SP", lambda e, og=og, par=par, wscr=wscr: e.dma_start(
                            out=wH[par][:].rearrange("p (i f) -> p i f", f=256),
                            in_=wscr[:, :, og * 256:(og + 1) * 256].transpose([1, 0, 2])),
                            reads=[wscrB], writes=[wHB[par]])
                        for oci in range(2):
                            oc = og * 2 + oci
                            for th in range(NTH):
                                bk, pk = pB[2 * oci + th % 2], pb[2 * oci + th % 2]
                                for ic in range(32):
                                    sch.pe(lambda e, par=par, oci=oci, ic=ic, th=th,
                                           pk=pk, src=src: e.matmul(
                                        pk[:, 0:512],
                                        wH[par][:, ic * 256 + oci * 128:
                                                ic * 256 + (oci + 1) * 128],
                                        src[:, ic * CT + th * 512:
                                            ic * CT + th * 512 + 512],
                                        start=(ic == 0), stop=(ic == 31)),
                                        reads=[wHB[par], srcB], writes=[bk])
                                sch.act(lambda e, oc=oc, th=th, pk=pk, dst=dst,
                                        s_col=s_col: e.activation(
                                    out=dst[:, oc * CT + th * 512:
                                            oc * CT + th * 512 + 512],
                                    in_=pk[:, 0:512], func=AF.Gelu,
                                    scale=s_col[:, oc:oc + 1]),
                                    reads=[bk, sB1], writes=[(dstB, (oc, th))])

                hstage(h1_scr, dB["h1_scr"], h_a, cB["h_a"], h_b, cB["h_b"],
                       s_h1s, gB["h1s"], h1s_e, gB["h1s_e"])
                hstage(h2_scr, dB["h2_scr"], h_b, cB["h_b"], h_a, cB["h_a"],
                       s_h2s, gB["h2s"], h2s_e, gB["h2s_e"])

                nsub = CT // 128
                for og in range(4):
                    par = og % 2
                    sch.dma("SP", lambda e, og=og, par=par: e.dma_start(
                        out=w3sb[par][:].rearrange("p (i f) -> p i f", f=256),
                        in_=w3_scr[:, :, og * 256:(og + 1) * 256].transpose([1, 0, 2])),
                        reads=[dB["w3_scr"]], writes=[w3B[par]])
                    sch.dma("SP", lambda e, og=og: e.dma_start(
                        out=w3sc[:], in_=dt["w3s_bc"][:, og * 256:(og + 1) * 256]),
                        writes=[gB["w3s_b"]])
                    for sq in range(nsub):
                        okey = (T0 // 512 + sq // 4, sq % 4)
                        bk, pk = pB[4 + sq % 2], pb[4 + sq % 2]
                        for ic in range(32):
                            sch.pe(lambda e, par=par, sq=sq, ic=ic, pk=pk: e.matmul(
                                pk[:, 0:256],
                                h_a[:, ic * CT + sq * 128:ic * CT + (sq + 1) * 128],
                                w3sb[par][:, ic * 256:(ic + 1) * 256],
                                start=(ic == 0), stop=(ic == 31)),
                                reads=[w3B[par], cB["h_a"]], writes=[bk])
                        sch.dve(lambda e, pk=pk: e.tensor_tensor(
                            out=outc[:], in0=pk[:, 0:256],
                            in1=w3sc[:], op=ALU.mult),
                            reads=[bk, gB["w3s_b"]], writes=[cB["outc"]])
                        sch.dma("POOL", lambda e, T0=T0, og=og, sq=sq: e.dma_start(
                            out=out_d[T0 + sq * 128:T0 + (sq + 1) * 128,
                                      og * 256:(og + 1) * 256],
                            in_=outc[:], accum_op=ALU.add),
                            reads=[cB["outc"]], writes=[(dB["out"], okey)])
                sch.flush(engmap)

            for tc in range(NSUPC if "C" in KPH else 0):
                phase_c_tile(tc)
            sch.flush(engmap)
            sch.barrier()
            sch.pool(lambda e: e.tensor_copy(cvb[0][:, 0:1], cvb[0][:, 0:1]))
            sch.flush(engmap)

    return nc



def make_inputs(full, core, N=4096, shared=None):
    if shared is not None:
        m = dict(shared)
        m["x"] = np.ascontiguousarray(np.asarray(full["L0_attn_x"][core], np.float32)[:N])
        return m
    T = lambda a: np.ascontiguousarray(np.asarray(a, np.float32).T)
    c = host_consts()
    m = {
        "x": np.ascontiguousarray(np.asarray(full["L0_attn_x"][core], np.float32)[:N]),
        "wqT": T(full["L0_attn_Wq"]), "wkT": T(full["L0_attn_Wk"]),
        "wvT": T(full["L0_attn_Wv"]), "wklT": T(full["L0_attn_Wkl"]),
        "wvlT": T(full["L0_attn_Wvl"]), "woT": T(full["L0_attn_Wo"]),
        "wgT": T(full["L0_attn_Wg"]),
        "wgoT": np.ascontiguousarray(
            np.asarray(full["L0_attn_Wgo"], np.float32).reshape(8, 128).T),
        "haarWkT": np.ascontiguousarray(
            np.transpose(np.asarray(full["L0_attn_haar_Wk"], np.float32), (0, 2, 1))),
        "haarWvT": np.ascontiguousarray(
            np.transpose(np.asarray(full["L0_attn_haar_Wv"], np.float32), (0, 2, 1))),
        "haar_scale": np.asarray(full["L0_attn_haar_scale"], np.float32).reshape(1, 4),
        "bg": np.ascontiguousarray(
            np.asarray(full["L0_attn_bg"], np.float32).reshape(8, 128).T),
        "bgo": np.asarray(full["L0_attn_bgo"], np.float32).reshape(1, 1),
        "bo_row": np.asarray(full["L0_attn_bo"], np.float32).reshape(1, DM),
        "w1T": T(full["L0_ffn_w1_w"]), "w2T": T(full["L0_ffn_w2_w"]),
        "h1T": T(full["L0_ffn_h1_w"]), "h2T": T(full["L0_ffn_h2_w"]),
        "w3T": T(full["L0_ffn_w3_w"]),
        "w1s": np.ascontiguousarray(
            np.asarray(full["L0_ffn_w1_s"], np.float32).reshape(32, 128).T),
        "w2s": np.ascontiguousarray(
            np.asarray(full["L0_ffn_w2_s"], np.float32).reshape(32, 128).T),
        "h1s": np.ascontiguousarray(
            np.asarray(full["L0_ffn_h1_s"], np.float32).reshape(32, 128).T),
        "h2s": np.ascontiguousarray(
            np.asarray(full["L0_ffn_h2_s"], np.float32).reshape(32, 128).T),
        "w3s_row": np.asarray(full["L0_ffn_w3_s"], np.float32).reshape(1, DM),
        "w3s_bc": np.ascontiguousarray(np.broadcast_to(
            np.asarray(full["L0_ffn_w3_s"], np.float32).reshape(1, DM), (128, DM))),
        "bo_bc": np.ascontiguousarray(np.broadcast_to(
            np.asarray(full["L0_attn_bo"], np.float32).reshape(1, DM), (128, DM))),
    }
    m.update({k: np.asarray(v) for k, v in c.items()})
    return m


def kernel(**inputs):
    N = 4096
    nc = build_program(N)
    shared = make_inputs(inputs, 0, N)
    in_maps = [make_inputs(inputs, core, N, shared=shared) for core in range(NCORES)]
    res = run_bass_kernel_spmd(nc, in_maps, core_ids=list(range(NCORES)))
    return np.stack([np.asarray(r["out"], np.float32) for r in res.results])
